# version 05
# baseline (speedup 1.0000x reference)
import numpy as np
import concourse.bass as bass
import concourse.mybir as mybir
from concourse.bass_utils import run_bass_kernel_spmd

F32 = mybir.dt.float32
BF16 = mybir.dt.bfloat16
AF = mybir.ActivationFunctionType
OP = mybir.AluOpType
AX = mybir.AxisListType

B, S, D = 4, 4096, 1024
HBUF = 3584
MP = 22
ENGINES = ("tensor", "vector", "scalar", "gpsimd", "sync")
ESEM = {"tensor": "T", "vector": "V", "scalar": "S", "gpsimd": "G", "sync": "Y"}


class Prog:
    def __init__(self):
        self.streams = {e: [] for e in ENGINES}
        self.cnt = {k: 0 for k in ("T", "V", "S", "G", "Y", "LD", "ST")}
        self.hw = {}
        self.last = None

    def step(self, eng, fn, sem=None, amt=1):
        sem = sem or ESEM[eng]
        waits = []
        t = self.last
        if t is not None:
            s, v = t
            if not (s == ESEM[eng]):
                k = (eng, s)
                if self.hw.get(k, 0) < v:
                    self.hw[k] = v
                    waits.append((s, v))
        self.cnt[sem] += amt
        tok = (sem, self.cnt[sem])
        self.streams[eng].append((fn, waits, sem, amt))
        self.last = tok
        return tok
    def fence(self, eng):
        sem = ESEM[eng]
        v = self.cnt[sem]
        if v > 0:
            self.streams[eng].append((None, [(sem, v)], sem, 0))



def build_nc(debug=False):
    nc = bass.Bass("TRN2", target_bir_lowering=False)
    p = Prog()

    def din(name, shape, dt=F32):
        return nc.dram_tensor(name, shape, dt, kind="ExternalInput")

    xk = din("xk", [S, D])
    kbias = din("kbias", [128, 32])
    fbias = din("fbias", [128, 28])
    tabs_in = din("tabs", [128, 4 * HBUF])
    vlen = din("vlen", [128, 1])
    eyef = din("eyef", [128, 128])
    onehots = din("onehots", [128, 32])
    onescol = din("onescol", [128, 1])
    bcast8 = din("bcast8", [8, 512])
    ones1x = din("ones1x", [1, 128])
    onesrow = din("onesrow", [1, 512])
    negio = din("negio", [128, 512])
    constv = din("constv", [128, 4])
    vmask = din("vmask", [128, 3072])
    wbig = [din(f"wbig{l}", [D, 1536]) for l in range(2)]
    wrot_in = [din(f"wrot{l}", [D, 1024]) for l in range(2)]
    wo_in = [din(f"wo{l}", [512, D]) for l in range(2)]
    w1p = [din(f"w1p{l}", [D, MP * 256]) for l in range(2)]
    w2p = [din(f"w2p{l}", [MP * 128, D]) for l in range(2)]
    b1c = [din(f"b1c{l}", [128, 2 * MP]) for l in range(2)]
    b2cin = [din(f"b2cin{l}", [128, 8]) for l in range(2)]
    lgrow = [din(f"lgrow{l}", [1, D]) for l in range(2)]
    lbrow = [din(f"lbrow{l}", [1, D]) for l in range(2)]
    qsc = [din(f"qsc{l}", [64, 1]) for l in range(2)]
    ksc = [din(f"ksc{l}", [64, 1]) for l in range(2)]
    wq_att = din("wq_att", [D, D])
    wk_att = din("wk_att", [D, D])
    wv_att = din("wv_att", [D, D])
    wcat_in = din("wcat", [D, D])
    bocin = din("bocin", [128, 8])
    out = nc.dram_tensor("out", [2048, D], F32, kind="ExternalOutput")

    kd = dict(kind="ExternalOutput") if debug else {}
    a1s = nc.dram_tensor("a1s", [8, 128, 3072], F32, **kd)
    hn1 = nc.dram_tensor("hn1", [8, 128, 3072], BF16, **kd)
    x2s = nc.dram_tensor("x2s", [8, 128, 3072], F32, **kd)
    prt = nc.dram_tensor("prt", [8, 128, 3072], F32)
    a2s = nc.dram_tensor("a2s", [8, 128, 3072], F32, **kd)
    hn2 = nc.dram_tensor("hn2", [8, 128, 2048], BF16)
    prt2 = nc.dram_tensor("prt2", [8, 128, 2048], F32)
    Fs = nc.dram_tensor("Fs", [8, 128, 2048], F32, **kd)
    xns = nc.dram_tensor("xns", [8, 128, 4096], BF16, **kd)
    oalls = nc.dram_tensor("oalls", [8, 128, 2048], BF16, **kd)

    AB = nc.alloc_sbuf_tensor("AB", [128, 56448], BF16)
    AFa = nc.alloc_sbuf_tensor("AFa", [128, 11784], F32)
    c_eyef = nc.alloc_sbuf_tensor("c_eyef", [128, 128], F32)
    c_oneh = nc.alloc_sbuf_tensor("c_oneh", [128, 32], F32)
    c_1col = nc.alloc_sbuf_tensor("c_1col", [128, 1], F32)
    c_bc8 = nc.alloc_sbuf_tensor("c_bc8", [8, 512], F32)
    c_1x = nc.alloc_sbuf_tensor("c_1x", [1, 128], F32)
    c_orow = nc.alloc_sbuf_tensor("c_orow", [1, 512], F32)
    c_kb = nc.alloc_sbuf_tensor("c_kb", [128, 32], F32)
    c_fb = nc.alloc_sbuf_tensor("c_fb", [128, 28], F32)
    c_ni = nc.alloc_sbuf_tensor("c_ni", [128, 512], F32)
    c_vl = nc.alloc_sbuf_tensor("c_vl", [128, 1], F32)
    rowst = nc.alloc_sbuf_tensor("rowst", [128, 8], F32)
    r_m = nc.alloc_sbuf_tensor("r_m", [1, 512], F32)
    r_a = nc.alloc_sbuf_tensor("r_a", [1, 512], F32)
    r_b = nc.alloc_sbuf_tensor("r_b", [1, 512], F32)
    r8a = nc.alloc_sbuf_tensor("r8a", [8, 512], F32)
    colA = nc.alloc_sbuf_tensor("colA", [128, 4], F32)
    lg_s = nc.alloc_sbuf_tensor("lg_s", [1, D], F32)
    lb_s = nc.alloc_sbuf_tensor("lb_s", [1, D], F32)
    b2_s = nc.alloc_sbuf_tensor("b2_s", [128, 8], F32)
    b1_s = nc.alloc_sbuf_tensor("b1_s", [128, 2 * MP], F32)
    bo_s = nc.alloc_sbuf_tensor("bo_s", [128, 8], F32)
    c_cv = nc.alloc_sbuf_tensor("c_cv", [128, 4], F32)
    c_vm = nc.alloc_sbuf_tensor("c_vm", [128, 512], F32)
    for _i, _v in enumerate([1e-5, 1e-24, 1e-30]):
        nc.const_aps.register(F32, _v, c_cv[:, _i:_i + 1])

    pbk = [nc.alloc_psum_tensor(f"pbk{i}", [128, 512], F32) for i in range(8)]

    class Arena:
        def __init__(self, t, n):
            self.t, self.n, self.off = t, n, 0

        def take(self, cols):
            o = self.off
            self.off += cols
            assert self.off <= self.n, f"arena overflow {self.off}>{self.n}"
            return self.t[:, o:o + cols]

        def reset(self):
            self.off = 0

    ab = Arena(AB, 56448)
    af = Arena(AFa, 11784)

    xnf = [af.take(513) for _ in range(8)]
    sqb = af.take(512)
    tmpb = af.take(512)
    rbsb_full = af.take(512)
    rbsb = rbsb_full[0:64, :]
    resb = [af.take(512) for _ in range(2)]
    xtm = af.take(1024)
    _shared = af.take(4096)
    qA = [_shared[:, 512 * i:512 * i + 512] for i in range(4)]
    tabt = [_shared[:, 2048 + 512 * i:2048 + 512 * i + 512] for i in range(4)]
    stgF = _shared[:, 0:4096]

    ab.reset()
    w_qkv = [ab.take(1536) for _ in range(8)]
    w_rot = [ab.take(1024) for _ in range(8)]
    w_wo = [ab.take(1024) for _ in range(4)]
    xnb = [ab.take(513) for _ in range(8)]
    kring = [ab.take(1024) for _ in range(4)]
    vring = ab.take(4160)
    qbuf = [ab.take(512) for _ in range(4)]
    ptile = [ab.take(512) for _ in range(4)]
    ofb = [ab.take(512) for _ in range(4)]
    ab.reset()
    w1h = [ab.take(2816) for _ in range(8)]
    w2h = ab.take(11264)
    hnb = [ab.take(512) for _ in range(8)]
    prodb = [ab.take(512) for _ in range(11)]
    gelb = ab.take(512)
    ab.reset()
    vall = ab.take(33280)
    xnt8 = [ab.take(512) for _ in range(8)]
    kpr = ab.take(4096)
    qpr = ab.take(2048)
    wv_a = [ab.take(1024) for _ in range(4)]
    wqk_a = [ab.take(256) for _ in range(8)]
    aptile = [ab.take(512) for _ in range(4)]
    oqt = [ab.take(512) for _ in range(8)]
    osm = ab.take(512)
    wcat_s = wv_a

    def T(fn):
        return p.step("tensor", fn)

    def V(fn):
        return p.step("vector", fn)

    def SC(fn):
        return p.step("scalar", fn)

    def G(fn):
        return p.step("gpsimd", fn)

    def LDMA(dst, src, slow=False):
        return p.step("sync", lambda e, d=dst, s=src, sl=slow: e.dma_start(out=d, in_=s, allow_slow_non_contiguous=sl), "LD", 16)

    def SDMA(dst, src):
        return p.step("gpsimd", lambda e, d=dst, s=src: e.dma_start(out=d, in_=s), "ST", 16)

    def rd(t, r0, r1, c0, c1):
        return t.ap()[r0:r1, c0:c1]

    LDMA(c_eyef[:, :], eyef.ap()[:, :])
    LDMA(c_oneh[:, :], onehots.ap()[:, :])
    LDMA(c_1col[:, :], onescol.ap()[:, :])
    LDMA(c_bc8[:, :], bcast8.ap()[:, :])
    LDMA(c_1x[:, :], ones1x.ap()[:, :])
    LDMA(c_orow[:, :], onesrow.ap()[:, :])
    LDMA(c_kb[:, :], kbias.ap()[:, :])
    LDMA(c_fb[:, :], fbias.ap()[:, :])
    LDMA(c_ni[:, :], negio.ap()[:, :])
    LDMA(c_vl[:, :], vlen.ap()[:, :])
    LDMA(bo_s[:, :], bocin.ap()[:, :])
    LDMA(c_cv[:, :], constv.ap()[:, :])

    def ln_group(src_rows_ap, dstf, dstb, dcol, want_f32):
        LDMA(xtm[:, :], src_rows_ap)
        V(lambda e: e.tensor_reduce(rowst[:, 0:1], xtm[:, :], AX.X, OP.add))
        V(lambda e: e.tensor_tensor(sqb[:, :], xtm[:, 0:512], xtm[:, 0:512], OP.mult))
        V(lambda e: e.tensor_reduce(rowst[:, 1:2], sqb[:, :], AX.X, OP.add))
        V(lambda e: e.tensor_tensor(sqb[:, :], xtm[:, 512:1024], xtm[:, 512:1024], OP.mult))
        V(lambda e: e.tensor_reduce(rowst[:, 2:3], sqb[:, :], AX.X, OP.add))
        p.fence("scalar")
        SC(lambda e: e.activation(rowst[:, 3:4], rowst[:, 0:1], AF.Identity, bias=0.0, scale=1.0 / D))
        p.fence("scalar")
        SC(lambda e: e.activation(rowst[:, 5:6], rowst[:, 1:2], AF.Identity, bias=rowst[:, 2:3], scale=1.0))
        p.fence("scalar")
        SC(lambda e: e.activation(rowst[:, 6:7], rowst[:, 3:4], AF.Square, bias=0.0, scale=1.0))
        p.fence("scalar")
        SC(lambda e: e.activation(rowst[:, 6:7], rowst[:, 6:7], AF.Identity, bias=0.0, scale=-1.0))
        p.fence("scalar")
        SC(lambda e: e.activation(rowst[:, 4:5], rowst[:, 5:6], AF.Identity, bias=rowst[:, 6:7], scale=1.0 / D))
        p.fence("scalar")
        SC(lambda e: e.activation(rowst[:, 5:6], rowst[:, 4:5], AF.Ln, bias=1e-5, scale=1.0))
        p.fence("scalar")
        SC(lambda e: e.activation(rowst[:, 4:5], rowst[:, 5:6], AF.Exp, bias=0.0, scale=-0.5))
        p.fence("scalar")
        SC(lambda e: e.activation(rowst[:, 6:7], rowst[:, 3:4], AF.Identity, bias=0.0, scale=rowst[:, 4:5]))
        SC(lambda e: e.activation(rowst[:, 6:7], rowst[:, 6:7], AF.Identity, bias=0.0, scale=-1.0))
        p.fence("scalar")
        SC(lambda e: e.activation(xtm[:, :], xtm[:, :], AF.Identity, bias=rowst[:, 6:7], scale=rowst[:, 4:5]))
        for c in range(8):
            bk = c % 2
            T(lambda e, cc=c, b=bk: e.transpose(pbk[b][:, 0:128], xtm[:, 128 * cc:128 * cc + 128], c_eyef[:, :]))
            if want_f32:
                V(lambda e, cc=c, b=bk: e.tensor_copy(dstf[cc][:, dcol:dcol + 128], pbk[b][:, 0:128]))
                G(lambda e, cc=c: e.tensor_copy(dstb[cc][:, dcol:dcol + 128], dstf[cc][:, dcol:dcol + 128]))
            else:
                V(lambda e, cc=c, b=bk: e.tensor_copy(dstb[cc][:, dcol:dcol + 128], pbk[b][:, 0:128]))

    def ffn_attn(l, src, dst, first_layer):
        b0 = 0 if first_layer else 512
        for c in range(8):
            LDMA(stgF[:, 0:1536], rd(wbig[l], 128 * c, 128 * c + 128, 0, 1536))
            G(lambda e, cc=c: e.tensor_copy(w_qkv[cc][:, :], stgF[:, 0:1536]))
        for c in range(8):
            LDMA(stgF[:, 0:1024], rd(wrot_in[l], 128 * c, 128 * c + 128, 0, 1024))
            G(lambda e, cc=c: e.tensor_copy(w_rot[cc][:, :], stgF[:, 0:1024]))
        for c in range(4):
            LDMA(stgF[:, 0:1024], rd(wo_in[l], 128 * c, 128 * c + 128, 0, D))
            G(lambda e, cc=c: e.tensor_copy(w_wo[cc][:, :], stgF[:, 0:1024]))
        LDMA(colA[0:64, 0:1], qsc[l].ap()[:, :])
        LDMA(colA[64:128, 0:1], qsc[l].ap()[:, :])
        LDMA(colA[0:32, 1:2], qsc[l].ap()[32:64, :])
        LDMA(colA[32:64, 1:2], qsc[l].ap()[0:32, :])
        LDMA(colA[64:96, 1:2], qsc[l].ap()[32:64, :])
        LDMA(colA[96:128, 1:2], qsc[l].ap()[0:32, :])
        LDMA(colA[0:64, 2:3], ksc[l].ap()[:, :])
        LDMA(colA[64:128, 2:3], ksc[l].ap()[:, :])
        LDMA(colA[0:32, 3:4], ksc[l].ap()[32:64, :])
        LDMA(colA[32:64, 3:4], ksc[l].ap()[0:32, :])
        LDMA(colA[64:96, 3:4], ksc[l].ap()[32:64, :])
        LDMA(colA[96:128, 3:4], ksc[l].ap()[0:32, :])
        G(lambda e: e.memset(vring[:, :], 1.0))

        ntiles = 7 if first_layer else 6
        for i in range(ntiles):
            tau = 512 * i
            s2 = i % 2
            for c in range(8):
                if i == 0:
                    G(lambda e, cc=c: e.memset(xnb[cc][:, 0:1], 0.0))
                else:
                    G(lambda e, cc=c: e.tensor_copy(xnb[cc][:, 0:1], xnb[cc][:, 512:513]))
            if first_layer:
                for s4 in range(4):
                    ln_group(rd(xk, 512 + tau + 128 * s4, 512 + tau + 128 * s4 + 128, 0, D),
                             xnf, xnb, 1 + 128 * s4, True)
            else:
                for c in range(8):
                    LDMA(xnf[c][:, 1:513], src.ap()[c, :, tau:tau + 512])
                    G(lambda e, cc=c: e.tensor_copy(xnb[cc][:, 1:513], xnf[cc][:, 1:513]))
            for ti in range(4):
                LDMA(tabt[ti][:, :], rd(tabs_in, 0, 128, ti * HBUF + b0 + tau, ti * HBUF + b0 + tau + 512))
                V(lambda e, t2=ti: e.tensor_scalar(tabt[t2][:, :], tabt[t2][:, :], colA[:, t2:t2 + 1], None, OP.mult))

            def rhs_x(c):
                return xnb[c][:, (1 if c < 4 else 0):(513 if c < 4 else 512)]

            def qk(which):
                cb = 0 if which == "q" else 512
                t1_, t2_ = (tabt[0], tabt[1]) if which == "q" else (tabt[2], tabt[3])
                for pp in range(4):
                    for c in range(8):
                        T(lambda e, cc=c, q=pp: e.matmul(pbk[q % 2][:, :], w_qkv[cc][:, cb + 128 * q:cb + 128 * q + 128], rhs_x(cc), start=(cc == 0), stop=(cc == 7)))
                    SC(lambda e, q=pp: e.copy(qA[q][:, :], pbk[q % 2][:, :]))
                    V(lambda e, q=pp: e.tensor_tensor(sqb[:, :], qA[q][:, :], qA[q][:, :], OP.mult))
                    T(lambda e, q=pp: e.matmul(pbk[2][0:8, :], c_oneh[:, 8 * q:8 * q + 8], sqb[:, :], start=(q == 0), stop=(q == 3)))
                SC(lambda e: e.activation(r8a[:, :], pbk[2][0:8, :], AF.Ln, bias=1e-24, scale=1.0))
                SC(lambda e: e.activation(r8a[:, :], r8a[:, :], AF.Exp, bias=0.0, scale=-0.5))
                for pp in range(4):
                    for c in range(8):
                        T(lambda e, cc=c, q=pp: e.matmul(pbk[4 + q][:, :], w_rot[cc][:, cb + 128 * q:cb + 128 * q + 128], rhs_x(cc), start=(cc == 0), stop=(cc == 7)))
                for pp in range(4):
                    T(lambda e, q=pp: e.matmul(pbk[3][:, :], c_bc8[:, 128 * q:128 * q + 128], r8a[:, :], start=True, stop=True))
                    V(lambda e, q=pp, ta=t1_: e.tensor_tensor(sqb[:, :], qA[q][:, :], ta[:, :], OP.mult))
                    V(lambda e, q=pp, tb=t2_: e.tensor_tensor(tmpb[:, :], pbk[4 + q][:, :], tb[:, :], OP.mult))
                    V(lambda e: e.tensor_add(sqb[:, :], sqb[:, :], tmpb[:, :]))
                    if which == "q":
                        V(lambda e, q=pp: e.tensor_tensor(qbuf[q][:, :], sqb[:, :], pbk[3][:, :], OP.mult))
                    else:
                        V(lambda e, q=pp, sl=s2: e.tensor_tensor(kring[q][:, 512 * sl:512 * sl + 512], sqb[:, :], pbk[3][:, :], OP.mult))

            qk("k")
            do_attn = (not first_layer) or (i >= 1)
            if do_attn:
                qk("q")
            for sub in range(4):
                for c in range(8):
                    off = (1 if c < 4 else 0) + 128 * sub
                    for pp in range(4):
                        T(lambda e, cc=c, s=sub, q=pp, o=off: e.matmul(pbk[4 + q][:, 128 * s:128 * s + 128], xnb[cc][:, o:o + 128], w_qkv[cc][:, 1024 + 128 * q:1024 + 128 * q + 128], start=(cc == 0), stop=(cc == 7)))
            for pp in range(4):
                dst520 = vring[:, 520 * s2:520 * s2 + 520] if False else None
                V(lambda e, q=pp, s=s2: e.tensor_copy(
                    kring[q][:, :][:, 0:0] if False else
                    vring[:, 520 * (2 * q + s):520 * (2 * q + s) + 520].rearrange("p (s4 h d) -> p s4 h d", s4=4, h=2, d=65)[:, :, :, 0:64],
                    pbk[4 + q][:, :].rearrange("p (s4 h d) -> p s4 h d", s4=4, h=2, d=64)))
            if (not first_layer):
                LDMA(c_vm[:, :], vmask.ap()[:, tau:tau + 512])
            if do_attn:
                clo = max(0, 4 * i - 4)
                chunks = list(range(clo, 4 * i + 4))
                for hh in range(8):
                    pp = hh // 2
                    half = hh % 2
                    h0 = 64 * half
                    for j0 in range(0, len(chunks), 4):
                        grp = chunks[j0:j0 + 4]
                        for j, ck in enumerate(grp):
                            T(lambda e, q=pp, hz=h0, sl=(ck // 4) % 2, k2=ck % 4, b=j:
                              e.matmul(pbk[b][:, :], kring[q][hz:hz + 64, 512 * sl + 128 * k2:512 * sl + 128 * k2 + 128], qbuf[q][hz:hz + 64, :], start=True, stop=True))
                        for j, ck in enumerate(grp):
                            SC(lambda e, b=j, bc=ck + b0 // 128: e.activation(ptile[b][:, :], pbk[b][:, :], AF.Exp, bias=c_fb[:, bc:bc + 1], scale=8.0))
                        for j, ck in enumerate(grp):
                            if ck >= 4 * i:
                                G(lambda e, b=j, bs=tau - 128 * ck: e.affine_select(out=ptile[b][:, :], in_=ptile[b][:, :], pattern=[[1, 512]], compare_op=OP.is_ge, fill=0.0, base=bs, channel_multiplier=-1))
                            else:
                                G(lambda e, b=j, bs=128 * ck - tau + 512: e.affine_select(out=ptile[b][:, :], in_=ptile[b][:, :], pattern=[[-1, 512]], compare_op=OP.is_ge, fill=0.0, base=bs, channel_multiplier=1))
                        for j, ck in enumerate(grp):
                            T(lambda e, q=pp, sl=(ck // 4) % 2, k2=ck % 4, h=half, b=j, st=(j0 == 0 and j == 0), sp=(j0 + j == len(chunks) - 1):
                              e.matmul(pbk[7][0:65, :], vring[:, 520 * (2 * q + sl) + 130 * k2 + 65 * h:520 * (2 * q + sl) + 130 * k2 + 65 * h + 65], ptile[b][:, :], start=st, stop=sp))
                    V(lambda e: e.tensor_copy(r_a[:, :], pbk[7][64:65, :]))
                    SC(lambda e: e.activation(r_b[:, :], r_a[:, :], AF.Ln, bias=1e-30, scale=1.0))
                    SC(lambda e: e.activation(r_b[:, :], r_b[:, :], AF.Exp, bias=0.0, scale=-1.0))
                    T(lambda e: e.matmul(pbk[6][0:64, :], c_1x[0:1, 0:64], r_b[:, :], start=True, stop=True))
                    V(lambda e: e.tensor_copy(rbsb[:, :], pbk[6][0:64, :]))
                    V(lambda e, c2=pp, h=half: e.tensor_tensor(ofb[c2][64 * h:64 * h + 64, :], pbk[7][0:64, :], rbsb[:, :], OP.mult))
                for c in range(8):
                    bk = c % 2
                    for ic in range(4):
                        T(lambda e, c2=ic, cc=c, b=bk: e.matmul(pbk[b][:, :], w_wo[c2][:, 128 * cc:128 * cc + 128], ofb[c2][:, :], start=(c2 == 0), stop=(c2 == 3)))
                    rs = c % 2
                    V(lambda e, cc=c, b=bk, r2=rs: e.tensor_add(resb[r2][:, :], pbk[b][:, :], xnf[cc][:, 1:513]))
                    if not first_layer:
                        V(lambda e, r2=rs: e.tensor_tensor(resb[r2][:, :], resb[r2][:, :], c_vm[:, :], OP.mult))
                    SDMA(dst.ap()[c, :, tau - (512 if first_layer else 0):tau - (512 if first_layer else 0) + 512], resb[rs][:, :])

    def mlp(l, srcscr, hnscr, prtscr, dstscr, jt, scol0, mask_dst):
        LDMA(lg_s[:, :], lgrow[l].ap()[:, :])
        LDMA(lb_s[:, :], lbrow[l].ap()[:, :])
        LDMA(b2_s[:, :], b2cin[l].ap()[:, :])
        LDMA(b1_s[:, :], b1c[l].ap()[:, :])
        for j in range(jt):
            sc = scol0 + 512 * j
            for c in range(8):
                LDMA(xnf[c][:, 1:513], srcscr.ap()[c, :, sc:sc + 512])
                if sc == 0:
                    G(lambda e, cc=c: e.memset(xnf[cc][:, 0:1], 0.0))
                else:
                    LDMA(xnf[c][:, 0:1], srcscr.ap()[c, :, sc - 1:sc], slow=True)

            def hsl(c):
                return xnf[c][:, (1 if c < 4 else 0):(513 if c < 4 else 512)]

            for c in range(8):
                V(lambda e, cc=c: e.tensor_tensor(sqb[:, :], hsl(cc), hsl(cc), OP.mult))
                T(lambda e, cc=c: e.matmul(pbk[2][0:1, :], c_1col[:, :], hsl(cc), start=(cc == 0), stop=(cc == 7)))
                T(lambda e, cc=c: e.matmul(pbk[5][0:1, :], c_1col[:, :], sqb[:, :], start=(cc == 0), stop=(cc == 7)))
            V(lambda e: e.tensor_scalar(r_m[:, :], pbk[2][0:1, :], 1.0 / D, None, OP.mult))
            V(lambda e: e.tensor_tensor(r_a[:, :], r_m[:, :], r_m[:, :], OP.mult))
            V(lambda e: e.scalar_tensor_tensor(r_a[:, :], pbk[5][0:1, :], 1.0 / D, r_a[:, :], OP.mult, OP.subtract))
            SC(lambda e: e.activation(r_b[:, :], r_a[:, :], AF.Ln, bias=1e-5, scale=1.0))
            SC(lambda e: e.activation(r_b[:, :], r_b[:, :], AF.Exp, bias=0.0, scale=-0.5))
            V(lambda e: e.scalar_tensor_tensor(r_a[:, :], r_m[:, :], -1.0, r_b[:, :], OP.mult, OP.mult))
            for c in range(8):
                T(lambda e, cc=c: e.matmul(pbk[3][:, :], lg_s[0:1, 128 * cc:128 * cc + 128], r_b[:, :], start=True, stop=True))
                T(lambda e, cc=c: e.matmul(pbk[4][:, :], lg_s[0:1, 128 * cc:128 * cc + 128], r_a[:, :], start=True, stop=False))
                T(lambda e, cc=c: e.matmul(pbk[4][:, :], lb_s[0:1, 128 * cc:128 * cc + 128], c_orow[:, :], start=False, stop=True))
                V(lambda e, cc=c: e.tensor_tensor(tmpb[:, :], hsl(cc), pbk[3][:, :], OP.mult))
                V(lambda e: e.tensor_add(sqb[:, :], tmpb[:, :], pbk[4][:, :]))
                G(lambda e, cc=c: e.tensor_copy(hnb[cc][:, :], sqb[:, :]))
                SDMA(hnscr.ap()[c, :, 512 * j:512 * j + 512], hnb[c][:, :])
        for half in range(2):
            for c in range(8):
                LDMA(stgF[:, 0:2816], rd(w1p[l], 128 * c, 128 * c + 128, 2816 * half, 2816 * half + 2816))
                G(lambda e, cc=c: e.tensor_copy(w1h[cc][:, :], stgF[:, 0:2816]))
            for mc in range(11):
                LDMA(stgF[:, 0:1024], rd(w2p[l], 128 * (11 * half + mc), 128 * (11 * half + mc) + 128, 0, D))
                G(lambda e, m=mc: e.tensor_copy(w2h[:, 1024 * m:1024 * m + 1024], stgF[:, 0:1024]))
            for j in range(jt):
                sc = scol0 + 512 * j
                if half == 1 and mask_dst:
                    LDMA(c_vm[:, :], vmask.ap()[:, 512 * j:512 * j + 512])
                for c in range(8):
                    LDMA(hnb[c][:, :], hnscr.ap()[c, :, 512 * j:512 * j + 512])
                for mc in range(11):
                    ma = 2 * (11 * half + mc)
                    for c in range(8):
                        T(lambda e, cc=c, m=mc: e.matmul(pbk[0][:, :], w1h[cc][:, 256 * m:256 * m + 128], hnb[cc][:, :], start=(cc == 0), stop=(cc == 7)))
                    for c in range(8):
                        T(lambda e, cc=c, m=mc: e.matmul(pbk[1][:, :], w1h[cc][:, 256 * m + 128:256 * m + 256], hnb[cc][:, :], start=(cc == 0), stop=(cc == 7)))
                    SC(lambda e, m2=ma: e.activation(gelb[:, :], pbk[1][:, :], AF.Gelu, bias=b1_s[:, m2 + 1:m2 + 2], scale=1.0))
                    V(lambda e, m=mc, m2=ma: e.scalar_tensor_tensor(prodb[m][:, :], pbk[0][:, :], b1_s[:, m2:m2 + 1], gelb[:, :], OP.add, OP.mult))
                for c in range(8):
                    bk = 4 + (c % 4)
                    for mc2 in range(11):
                        T(lambda e, m=mc2, cc=c, b=bk: e.matmul(pbk[b][:, :], w2h[:, 1024 * m + 128 * cc:1024 * m + 128 * cc + 128], prodb[m][:, :], start=(m == 0), stop=(m == 10)))
                    ss = c % 2
                    if half == 0:
                        V(lambda e, b=bk, s=ss: e.tensor_copy(resb[s][:, :], pbk[b][:, :]))
                        SDMA(prtscr.ap()[c, :, 512 * j:512 * j + 512], resb[ss][:, :])
                    else:
                        LDMA(tmpb[:, :], prtscr.ap()[c, :, 512 * j:512 * j + 512])
                        LDMA(sqb[:, :], srcscr.ap()[c, :, sc:sc + 512])
                        V(lambda e, b=bk, s=ss: e.tensor_add(resb[s][:, :], pbk[b][:, :], tmpb[:, :]))
                        V(lambda e, s=ss: e.tensor_add(resb[s][:, :], resb[s][:, :], sqb[:, :]))
                        V(lambda e, s=ss, cc=c: e.tensor_scalar(resb[s][:, :], resb[s][:, :], b2_s[:, cc:cc + 1], None, OP.add))
                        if mask_dst:
                            V(lambda e, s=ss: e.tensor_tensor(resb[s][:, :], resb[s][:, :], c_vm[:, :], OP.mult))
                        SDMA(dstscr.ap()[c, :, 512 * j:512 * j + 512], resb[ss][:, :])

    def attn_phase():
        G(lambda e: e.memset(vall[:, :], 1.0))
        for ph in range(2):
            for c in range(8):
                LDMA(stgF[:, 0:512], rd(wv_att, 128 * c, 128 * c + 128, 512 * ph, 512 * ph + 512))
                G(lambda e, cc=c: e.tensor_copy(wv_h[cc][:, :], stgF[:, 0:512]))
            for tg in range(8):
                if ph == 0:
                    for s4 in range(4):
                        ln_group(rd(xk, 512 * tg + 128 * s4, 512 * tg + 128 * s4 + 128, 0, D), None, xnt8, 128 * s4, False)
                    for c in range(8):
                        SDMA(xns.ap()[c, :, 512 * tg:512 * tg + 512], xnt8[c][:, :])
                else:
                    for c in range(8):
                        LDMA(xnt8[c][:, :], xns.ap()[c, :, 512 * tg:512 * tg + 512])
                for sub in range(4):
                    for c in range(8):
                        T(lambda e, cc=c, s=sub: e.matmul(pbk[4][:, :], xnt8[cc][:, 128 * s:128 * s + 128], wv_h[cc][:, :], start=(cc == 0), stop=(cc == 7)))
                    tc = 4 * tg + sub
                    V(lambda e, t2=tc, p2=ph: e.tensor_copy(
                        vall[:, 1040 * t2 + 520 * p2:1040 * t2 + 520 * p2 + 520].rearrange("p (q h d) -> p q h d", q=4, h=2, d=65)[:, :, :, 0:64],
                        pbk[4][:, :].rearrange("p (q h d) -> p q h d", q=4, h=2, d=64)))
        for pp in range(8):
            for c in range(8):
                LDMA(stgF[:, 0:128], rd(wq_att, 128 * c, 128 * c + 128, 128 * pp, 128 * pp + 128))
                LDMA(stgF[:, 128:256], rd(wk_att, 128 * c, 128 * c + 128, 128 * pp, 128 * pp + 128))
                G(lambda e, cc=c: e.tensor_copy(wqk_a[cc][:, :], stgF[:, 0:256]))
            for kt in range(8):
                for c in range(8):
                    LDMA(xnt8[c][:, :], xns.ap()[c, :, 512 * kt:512 * kt + 512])
                for c in range(8):
                    T(lambda e, cc=c: e.matmul(pbk[0][:, :], wqk_a[cc][:, 128:256], xnt8[cc][:, :], start=(cc == 0), stop=(cc == 7)))
                SC(lambda e, k2=kt: e.copy(kpr[:, 512 * k2:512 * k2 + 512], pbk[0][:, :]))
                if kt >= 4:
                    for c in range(8):
                        T(lambda e, cc=c: e.matmul(pbk[1][:, :], wqk_a[cc][:, 0:128], xnt8[cc][:, :], start=(cc == 0), stop=(cc == 7)))
                    SC(lambda e, k2=kt - 4: e.copy(qpr[:, 512 * k2:512 * k2 + 512], pbk[1][:, :]))
            for hh in range(2):
                h0 = 64 * hh
                for qt in range(4):
                    kmax = 20 + 4 * qt
                    for j0 in range(0, kmax, 4):
                        grp = list(range(j0, min(j0 + 4, kmax)))
                        for j, ck in enumerate(grp):
                            T(lambda e, k2=ck, b=j, h=h0, q=qt: e.matmul(pbk[b][:, :], kpr[h:h + 64, 128 * k2:128 * k2 + 128], qpr[h:h + 64, 512 * q:512 * q + 512], start=True, stop=True))
                        for j, ck in enumerate(grp):
                            SC(lambda e, b=j, k2=ck: e.activation(aptile[b][:, :], pbk[b][:, :], AF.Exp, bias=c_kb[:, k2:k2 + 1], scale=0.125))
                        for j, ck in enumerate(grp):
                            if ck >= kmax - 4:
                                G(lambda e, b=j, bs=2048 + 512 * qt - 128 * ck: e.affine_select(out=aptile[b][:, :], in_=aptile[b][:, :], pattern=[[1, 512]], compare_op=OP.is_ge, fill=0.0, base=bs, channel_multiplier=-1))
                        for j, ck in enumerate(grp):
                            T(lambda e, k2=ck, h=hh, b=j, q=pp, st=(j0 == 0 and j == 0), sp=(j0 + j == kmax - 1):
                              e.matmul(pbk[7][0:65, :], vall[:, 1040 * k2 + 520 * (q // 4) + 130 * (q % 4) + 65 * h:1040 * k2 + 520 * (q // 4) + 130 * (q % 4) + 65 * h + 65], aptile[b][:, :], start=st, stop=sp))
                    V(lambda e: e.tensor_copy(r_a[:, :], pbk[7][64:65, :]))
                    SC(lambda e: e.activation(r_b[:, :], r_a[:, :], AF.Ln, bias=1e-30, scale=1.0))
                    SC(lambda e: e.activation(r_b[:, :], r_b[:, :], AF.Exp, bias=0.0, scale=-1.0))
                    T(lambda e: e.matmul(pbk[6][0:64, :], c_1x[0:1, 0:64], r_b[:, :], start=True, stop=True))
                    V(lambda e: e.tensor_copy(rbsb[:, :], pbk[6][0:64, :]))
                    V(lambda e, h=h0: e.tensor_tensor(osm[h:h + 64, :], pbk[7][0:64, :], rbsb[:, :], OP.mult))
                    SDMA(oalls.ap()[pp, 64 * hh:64 * hh + 64, 512 * qt:512 * qt + 512], osm[64 * hh:64 * hh + 64, :])
        for qt in range(4):
            for ic in range(8):
                LDMA(oqt[ic][:, :], oalls.ap()[ic, :, 512 * qt:512 * qt + 512])
            for wh in range(2):
                for c2 in range(4):
                    cc = 4 * wh + c2
                    LDMA(stgF[:, 0:1024], rd(wcat_in, 128 * cc, 128 * cc + 128, 0, D))
                    G(lambda e, c3=c2: e.tensor_copy(wcat_s[c3][:, :], stgF[:, 0:1024]))
                for c in range(8):
                    for ic in range(4):
                        T(lambda e, c2=ic, cc=c, b=c, w=wh: e.matmul(pbk[b][:, :], wcat_s[c2][:, 128 * cc:128 * cc + 128], oqt[4 * w + c2][:, :], start=(c2 == 0 and w == 0), stop=(c2 == 3 and w == 1)))
                if wh == 0:
                    continue
                for c in range(8):
                    LDMA(tmpb[:, :], Fs.ap()[c, :, 512 * qt:512 * qt + 512])
                    V(lambda e, b=c, cc=c: e.scalar_tensor_tensor(xnf[cc][:, 0:512], pbk[b][:, :], bo_s[:, cc:cc + 1], tmpb[:, :], OP.add, OP.add))
            for s4 in range(4):
                for dmg in range(2):
                    for c2 in range(4):
                        cc = 4 * dmg + c2
                        T(lambda e, c3=cc, s=s4, b=2 + dmg, c4=c2: e.transpose(pbk[b][:, 128 * c4:128 * c4 + 128], xnf[c3][:, 128 * s:128 * s + 128], c_eyef[:, :]))
                LDMA(xtm[:, :], rd(xk, 2048 + 512 * qt + 128 * s4, 2048 + 512 * qt + 128 * s4 + 128, 0, D))
                V(lambda e: e.tensor_add(resb[0][:, :], pbk[2][:, :], xtm[:, 0:512]))
                V(lambda e: e.tensor_add(resb[1][:, :], pbk[3][:, :], xtm[:, 512:1024]))
                SDMA(out.ap()[512 * qt + 128 * s4:512 * qt + 128 * s4 + 128, 0:512], resb[0][:, :])
                SDMA(out.ap()[512 * qt + 128 * s4:512 * qt + 128 * s4 + 128, 512:1024], resb[1][:, :])

    wv_h = [wv_a[i // 2][:, 512 * (i % 2):512 * (i % 2) + 512] for i in range(8)]

    ffn_attn(0, None, a1s, True)
    mlp(0, a1s, hn1, prt, x2s, 6, 0, True)
    ffn_attn(1, x2s, a2s, False)
    mlp(1, a2s, hn2, prt2, Fs, 4, 1024, False)
    attn_phase()
    p.step("sync", None)

    import contextlib
    stack = contextlib.ExitStack()
    sems = {}
    for s in ("T", "V", "S", "G", "Y", "LD", "ST"):
        sems[s] = stack.enter_context(nc.semaphore(f"sem{s}"))
    blk = stack.enter_context(nc.Block())

    def mk(engname):
        def body(eng):
            for fn, waits, sem, amt in p.streams[engname]:
                for (s, v) in waits:
                    eng.wait_ge(sems[s], v)
                if fn is not None:
                    fn(eng).then_inc(sems[sem], amt)
        return body

    blk.tensor(mk("tensor"))
    blk.vector(mk("vector"))
    blk.scalar(mk("scalar"))
    blk.gpsimd(mk("gpsimd"))
    blk.sync(mk("sync"))
    stack.close()
    return nc


def _tables(core_h):
    toff = 512 if core_h == 1 else -1536
    t = np.arange(HBUF, dtype=np.float64) + toff
    d = np.arange(64)
    half = d % 32
    inv_freq = 1.0 / (10000.0 ** (2 * half / 64.0))
    freqs = t[:, None] * inv_freq[None, :]
    base = (2 * half + 0.4 * 64) / (1.4 * 64)
    power = (t - 2048.0) / 256.0
    xsc = base[None, :] ** power[:, None]
    sgn = np.where(d < 32, -1.0, 1.0)
    t1q = np.cos(freqs) * xsc
    t2q = np.sin(freqs) * xsc * sgn[None, :]
    t1k = np.cos(freqs) / xsc
    t2k = np.sin(freqs) / xsc * sgn[None, :]
    tab = np.concatenate([t1q.T, t2q.T, t1k.T, t2k.T], axis=1).astype(np.float32)
    return np.ascontiguousarray(np.concatenate([tab, tab], axis=0))


_NC_CACHE = {}
_LAST_RES = None


def kernel(L0_attn_resid, L0_attn_W_Q, L0_attn_W_K, L0_attn_W_V, L0_attn_W_out,
           L0_attn_b_out, L0_ffn_wqkv, L0_ffn_q_scale, L0_ffn_k_scale, L0_ffn_wo,
           L0_ffn_ln_g, L0_ffn_ln_b, L0_ffn_w1, L0_ffn_b1, L0_ffn_w2, L0_ffn_b2):
    x = np.ascontiguousarray(np.asarray(L0_attn_resid, np.float32))
    f32 = lambda a: np.ascontiguousarray(np.asarray(a, np.float32))
    sh = {}
    sh["eyef"] = np.eye(128, dtype=np.float32)
    oneh = np.zeros((128, 32), np.float32)
    for pp in range(4):
        oneh[0:64, 8 * pp + 2 * pp] = 1.0
        oneh[64:128, 8 * pp + 2 * pp + 1] = 1.0
    sh["onehots"] = oneh
    sh["onescol"] = np.ones((128, 1), np.float32)
    bc8 = np.zeros((8, 512), np.float32)
    for pp in range(4):
        bc8[2 * pp, 128 * pp:128 * pp + 64] = 1.0
        bc8[2 * pp + 1, 128 * pp + 64:128 * pp + 128] = 1.0
    sh["bcast8"] = bc8
    sh["ones1x"] = np.ones((1, 128), np.float32)
    sh["onesrow"] = np.ones((1, 512), np.float32)
    sh["negio"] = np.tile(-np.arange(512, dtype=np.float32)[None, :], (128, 1))
    sh["constv"] = np.tile(np.array([1e-5, 1e-24, 1e-30, 0.0], np.float32)[None, :], (128, 1))
    wqkv = f32(L0_ffn_wqkv)
    perm = (np.arange(64) + 32) % 64
    for l in range(2):
        sh[f"wbig{l}"] = np.ascontiguousarray(wqkv[l])
        rot = np.zeros((D, 1024), np.float32)
        rot[:, 0:512] = wqkv[l][:, 0:512].reshape(D, 8, 64)[:, :, perm].reshape(D, 512)
        rot[:, 512:1024] = wqkv[l][:, 512:1024].reshape(D, 8, 64)[:, :, perm].reshape(D, 512)
        sh[f"wrot{l}"] = rot
        sh[f"wo{l}"] = f32(L0_ffn_wo)[l]
        w1 = f32(L0_ffn_w1)[l]
        b1 = f32(L0_ffn_b1)[l]
        a = np.zeros((D, MP * 128), np.float32)
        g = np.zeros((D, MP * 128), np.float32)
        a[:, 0:2730] = w1[:, 0:2730]
        g[:, 0:2730] = w1[:, 2730:5460]
        ba = np.zeros(MP * 128, np.float32)
        bg = np.zeros(MP * 128, np.float32)
        ba[0:2730] = b1[0:2730]
        bg[0:2730] = b1[2730:5460]
        w1pp = np.zeros((D, MP * 256), np.float32)
        b1cc = np.zeros((128, 2 * MP), np.float32)
        for m in range(MP):
            w1pp[:, 256 * m:256 * m + 128] = a[:, 128 * m:128 * m + 128]
            w1pp[:, 256 * m + 128:256 * m + 256] = g[:, 128 * m:128 * m + 128]
            b1cc[:, 2 * m] = ba[128 * m:128 * m + 128]
            b1cc[:, 2 * m + 1] = bg[128 * m:128 * m + 128]
        sh[f"w1p{l}"] = w1pp
        sh[f"b1c{l}"] = b1cc
        w2pp = np.zeros((MP * 128, D), np.float32)
        w2pp[0:2730, :] = f32(L0_ffn_w2)[l]
        sh[f"w2p{l}"] = w2pp
        sh[f"b2cin{l}"] = np.ascontiguousarray(f32(L0_ffn_b2)[l].reshape(8, 128).T)
        sh[f"lgrow{l}"] = f32(L0_ffn_ln_g)[l][None, :]
        sh[f"lbrow{l}"] = f32(L0_ffn_ln_b)[l][None, :]
        sh[f"qsc{l}"] = f32(L0_ffn_q_scale)[l][:, None]
        sh[f"ksc{l}"] = f32(L0_ffn_k_scale)[l][:, None]
    sh["wq_att"] = f32(L0_attn_W_Q).reshape(D, D)
    sh["wk_att"] = f32(L0_attn_W_K).reshape(D, D)
    sh["wv_att"] = f32(L0_attn_W_V).reshape(D, D)
    sh["wcat"] = np.ascontiguousarray(f32(L0_attn_W_out).transpose(1, 0, 2).reshape(D, D))
    sh["bocin"] = np.ascontiguousarray(f32(L0_attn_b_out).reshape(8, 128).T)

    tab_lo, tab_hi = _tables(0), _tables(1)
    kb_lo = np.full((128, 32), -24.0, np.float32)
    kb_lo[:, 0:16] = -1e9
    kb_hi = np.full((128, 32), -24.0, np.float32)
    fb_lo = np.full((128, 28), -32.0, np.float32)
    fb_lo[:, 0:12] = -1e9
    fb_hi = np.full((128, 28), -32.0, np.float32)
    vl_lo = np.full((128, 1), -1535.5, np.float32)
    vl_hi = np.full((128, 1), 1e9, np.float32)

    if "nc" not in _NC_CACHE:
        _NC_CACHE["nc"] = build_nc()
    nc = _NC_CACHE["nc"]
    in_maps = []
    for core in range(8):
        b, h = core // 2, core % 2
        xkc = x[b] if h == 1 else np.concatenate([np.zeros((2048, D), np.float32), x[b, 0:2048]], axis=0)
        vm = np.ones((128, 3072), np.float32)
        if h == 0:
            vm[:, 0:1024] = 0.0
        m = {"vmask": vm,
             "xk": np.ascontiguousarray(xkc),
             "kbias": kb_hi if h == 1 else kb_lo,
             "fbias": fb_hi if h == 1 else fb_lo,
             "tabs": tab_hi if h == 1 else tab_lo,
             "vlen": vl_hi if h == 1 else vl_lo}
        m.update(sh)
        in_maps.append(m)
    res = run_bass_kernel_spmd(nc, in_maps, core_ids=list(range(8)))
    global _LAST_RES
    _LAST_RES = res
    o = np.zeros((B, S, D), np.float32)
    for core in range(8):
        b, h = core // 2, core % 2
        o[b, 2048 * h:2048 * h + 2048] = res.results[core]["out"]
    return o


# version 06
# speedup vs baseline: 1.1324x; 1.1324x over previous
import numpy as np
import concourse.bass as bass
import concourse.mybir as mybir
from concourse.bass_utils import run_bass_kernel_spmd

F32 = mybir.dt.float32
BF16 = mybir.dt.bfloat16
AF = mybir.ActivationFunctionType
OP = mybir.AluOpType
AX = mybir.AxisListType

B, S, D = 4, 4096, 1024
HBUF = 3584
MP = 22
ENGINES = ("tensor", "vector", "scalar", "gpsimd", "sync")
ESEM = {"tensor": "T", "vector": "V", "scalar": "S", "gpsimd": "G", "sync": "Y"}


class Prog:
    def __init__(self):
        self.streams = {e: [] for e in ENGINES}
        self.cnt = {k: 0 for k in ("T", "V", "S", "G", "Y", "LD", "ST")}
        self.hw = {}
        self.last = None

    def step(self, eng, fn, sem=None, amt=1):
        sem = sem or ESEM[eng]
        waits = []
        t = self.last
        if t is not None:
            s, v = t
            if not (s == ESEM[eng]) and not (sem in ("LD", "ST") and s == sem):
                k = (eng, s)
                if self.hw.get(k, 0) < v:
                    self.hw[k] = v
                    waits.append((s, v))
        self.cnt[sem] += amt
        tok = (sem, self.cnt[sem])
        self.streams[eng].append((fn, waits, sem, amt))
        self.last = tok
        return tok
    def fence(self, eng):
        sem = ESEM[eng]
        v = self.cnt[sem]
        if v > 0:
            self.streams[eng].append((None, [(sem, v)], sem, 0))



def build_nc(debug=False):
    nc = bass.Bass("TRN2", target_bir_lowering=False)
    p = Prog()

    def din(name, shape, dt=F32):
        return nc.dram_tensor(name, shape, dt, kind="ExternalInput")

    xk = din("xk", [S, D])
    kbias = din("kbias", [128, 32])
    fbias = din("fbias", [128, 28])
    tabs_in = din("tabs", [128, 4 * HBUF])
    vlen = din("vlen", [128, 1])
    eyef = din("eyef", [128, 128])
    onehots = din("onehots", [128, 32])
    onescol = din("onescol", [128, 1])
    bcast8 = din("bcast8", [8, 512])
    ones1x = din("ones1x", [1, 128])
    onesrow = din("onesrow", [1, 512])
    negio = din("negio", [128, 512])
    constv = din("constv", [128, 4])
    vmask = din("vmask", [128, 3072])
    wbig = [din(f"wbig{l}", [D, 1536]) for l in range(2)]
    wrot_in = [din(f"wrot{l}", [D, 1024]) for l in range(2)]
    wo_in = [din(f"wo{l}", [512, D]) for l in range(2)]
    w1p = [din(f"w1p{l}", [D, MP * 256]) for l in range(2)]
    w2p = [din(f"w2p{l}", [MP * 128, D]) for l in range(2)]
    b1c = [din(f"b1c{l}", [128, 2 * MP]) for l in range(2)]
    b2cin = [din(f"b2cin{l}", [128, 8]) for l in range(2)]
    lgrow = [din(f"lgrow{l}", [1, D]) for l in range(2)]
    lbrow = [din(f"lbrow{l}", [1, D]) for l in range(2)]
    qsc = [din(f"qsc{l}", [64, 1]) for l in range(2)]
    ksc = [din(f"ksc{l}", [64, 1]) for l in range(2)]
    wq_att = din("wq_att", [D, D])
    wk_att = din("wk_att", [D, D])
    wv_att = din("wv_att", [D, D])
    wcat_in = din("wcat", [D, D])
    bocin = din("bocin", [128, 8])
    out = nc.dram_tensor("out", [2048, D], F32, kind="ExternalOutput")

    kd = dict(kind="ExternalOutput") if debug else {}
    a1s = nc.dram_tensor("a1s", [8, 128, 3072], F32, **kd)
    hn1 = nc.dram_tensor("hn1", [8, 128, 3072], BF16, **kd)
    x2s = nc.dram_tensor("x2s", [8, 128, 3072], F32, **kd)
    prt = nc.dram_tensor("prt", [8, 128, 3072], F32)
    a2s = nc.dram_tensor("a2s", [8, 128, 3072], F32, **kd)
    hn2 = nc.dram_tensor("hn2", [8, 128, 2048], BF16)
    prt2 = nc.dram_tensor("prt2", [8, 128, 2048], F32)
    Fs = nc.dram_tensor("Fs", [8, 128, 2048], F32, **kd)
    xns = nc.dram_tensor("xns", [8, 128, 4096], BF16, **kd)
    oalls = nc.dram_tensor("oalls", [8, 128, 2048], BF16, **kd)

    AB = nc.alloc_sbuf_tensor("AB", [128, 56448], BF16)
    AFa = nc.alloc_sbuf_tensor("AFa", [128, 11784], F32)
    c_eyef = nc.alloc_sbuf_tensor("c_eyef", [128, 128], F32)
    c_oneh = nc.alloc_sbuf_tensor("c_oneh", [128, 32], F32)
    c_1col = nc.alloc_sbuf_tensor("c_1col", [128, 1], F32)
    c_bc8 = nc.alloc_sbuf_tensor("c_bc8", [8, 512], F32)
    c_1x = nc.alloc_sbuf_tensor("c_1x", [1, 128], F32)
    c_orow = nc.alloc_sbuf_tensor("c_orow", [1, 512], F32)
    c_kb = nc.alloc_sbuf_tensor("c_kb", [128, 32], F32)
    c_fb = nc.alloc_sbuf_tensor("c_fb", [128, 28], F32)
    c_ni = nc.alloc_sbuf_tensor("c_ni", [128, 512], F32)
    c_vl = nc.alloc_sbuf_tensor("c_vl", [128, 1], F32)
    rowst = nc.alloc_sbuf_tensor("rowst", [128, 8], F32)
    r_m = nc.alloc_sbuf_tensor("r_m", [1, 512], F32)
    r_a = nc.alloc_sbuf_tensor("r_a", [1, 512], F32)
    r_b = nc.alloc_sbuf_tensor("r_b", [1, 512], F32)
    r8a = nc.alloc_sbuf_tensor("r8a", [8, 512], F32)
    colA = nc.alloc_sbuf_tensor("colA", [128, 4], F32)
    lg_s = nc.alloc_sbuf_tensor("lg_s", [1, D], F32)
    lb_s = nc.alloc_sbuf_tensor("lb_s", [1, D], F32)
    b2_s = nc.alloc_sbuf_tensor("b2_s", [128, 8], F32)
    b1_s = nc.alloc_sbuf_tensor("b1_s", [128, 2 * MP], F32)
    bo_s = nc.alloc_sbuf_tensor("bo_s", [128, 8], F32)
    c_cv = nc.alloc_sbuf_tensor("c_cv", [128, 4], F32)
    c_vm = nc.alloc_sbuf_tensor("c_vm", [128, 512], F32)
    for _i, _v in enumerate([1e-5, 1e-24, 1e-30]):
        nc.const_aps.register(F32, _v, c_cv[:, _i:_i + 1])

    pbk = [nc.alloc_psum_tensor(f"pbk{i}", [128, 512], F32) for i in range(8)]

    class Arena:
        def __init__(self, t, n):
            self.t, self.n, self.off = t, n, 0

        def take(self, cols):
            o = self.off
            self.off += cols
            assert self.off <= self.n, f"arena overflow {self.off}>{self.n}"
            return self.t[:, o:o + cols]

        def reset(self):
            self.off = 0

    ab = Arena(AB, 56448)
    af = Arena(AFa, 11784)

    xnf = [af.take(513) for _ in range(8)]
    sqb = af.take(512)
    tmpb = af.take(512)
    rbsb_full = af.take(512)
    rbsb = rbsb_full[0:64, :]
    resb = [af.take(512) for _ in range(2)]
    xtm = af.take(1024)
    _shared = af.take(4096)
    qA = [_shared[:, 512 * i:512 * i + 512] for i in range(4)]
    tabt = [_shared[:, 2048 + 512 * i:2048 + 512 * i + 512] for i in range(4)]
    stgF = _shared[:, 0:4096]

    ab.reset()
    w_qkv = [ab.take(1536) for _ in range(8)]
    w_rot = [ab.take(1024) for _ in range(8)]
    w_wo = [ab.take(1024) for _ in range(4)]
    xnb = [ab.take(513) for _ in range(8)]
    kring = [ab.take(1024) for _ in range(4)]
    vring = ab.take(4160)
    qbuf = [ab.take(512) for _ in range(4)]
    ptile = [ab.take(512) for _ in range(4)]
    ofb = [ab.take(512) for _ in range(4)]
    ab.reset()
    w1h = [ab.take(2816) for _ in range(8)]
    w2h = ab.take(11264)
    hnb = [ab.take(512) for _ in range(8)]
    prodb = [ab.take(512) for _ in range(11)]
    gelb = ab.take(512)
    ab.reset()
    vall = ab.take(33280)
    xnt8 = [ab.take(512) for _ in range(8)]
    kpr = ab.take(4096)
    qpr = ab.take(2048)
    wv_a = [ab.take(1024) for _ in range(4)]
    wqk_a = [ab.take(256) for _ in range(8)]
    aptile = [ab.take(512) for _ in range(4)]
    oqt = [ab.take(512) for _ in range(8)]
    osm = ab.take(512)
    wcat_s = wv_a

    def T(fn):
        return p.step("tensor", fn)

    def V(fn):
        return p.step("vector", fn)

    def SC(fn):
        return p.step("scalar", fn)

    def G(fn):
        return p.step("gpsimd", fn)

    def LDMA(dst, src, slow=False):
        return p.step("sync", lambda e, d=dst, s=src, sl=slow: e.dma_start(out=d, in_=s, allow_slow_non_contiguous=sl), "LD", 16)

    def SDMA(dst, src):
        return p.step("gpsimd", lambda e, d=dst, s=src: e.dma_start(out=d, in_=s), "ST", 16)

    def rd(t, r0, r1, c0, c1):
        return t.ap()[r0:r1, c0:c1]

    LDMA(c_eyef[:, :], eyef.ap()[:, :])
    LDMA(c_oneh[:, :], onehots.ap()[:, :])
    LDMA(c_1col[:, :], onescol.ap()[:, :])
    LDMA(c_bc8[:, :], bcast8.ap()[:, :])
    LDMA(c_1x[:, :], ones1x.ap()[:, :])
    LDMA(c_orow[:, :], onesrow.ap()[:, :])
    LDMA(c_kb[:, :], kbias.ap()[:, :])
    LDMA(c_fb[:, :], fbias.ap()[:, :])
    LDMA(c_ni[:, :], negio.ap()[:, :])
    LDMA(c_vl[:, :], vlen.ap()[:, :])
    LDMA(bo_s[:, :], bocin.ap()[:, :])
    LDMA(c_cv[:, :], constv.ap()[:, :])

    def ln_group(src_rows_ap, dstf, dstb, dcol, want_f32):
        LDMA(xtm[:, :], src_rows_ap)
        V(lambda e: e.tensor_reduce(rowst[:, 0:1], xtm[:, :], AX.X, OP.add))
        V(lambda e: e.tensor_tensor(sqb[:, :], xtm[:, 0:512], xtm[:, 0:512], OP.mult))
        V(lambda e: e.tensor_reduce(rowst[:, 1:2], sqb[:, :], AX.X, OP.add))
        V(lambda e: e.tensor_tensor(sqb[:, :], xtm[:, 512:1024], xtm[:, 512:1024], OP.mult))
        V(lambda e: e.tensor_reduce(rowst[:, 2:3], sqb[:, :], AX.X, OP.add))
        p.fence("scalar")
        SC(lambda e: e.activation(rowst[:, 3:4], rowst[:, 0:1], AF.Identity, bias=0.0, scale=1.0 / D))
        p.fence("scalar")
        SC(lambda e: e.activation(rowst[:, 5:6], rowst[:, 1:2], AF.Identity, bias=rowst[:, 2:3], scale=1.0))
        p.fence("scalar")
        SC(lambda e: e.activation(rowst[:, 6:7], rowst[:, 3:4], AF.Square, bias=0.0, scale=1.0))
        p.fence("scalar")
        SC(lambda e: e.activation(rowst[:, 6:7], rowst[:, 6:7], AF.Identity, bias=0.0, scale=-1.0))
        p.fence("scalar")
        SC(lambda e: e.activation(rowst[:, 4:5], rowst[:, 5:6], AF.Identity, bias=rowst[:, 6:7], scale=1.0 / D))
        p.fence("scalar")
        SC(lambda e: e.activation(rowst[:, 5:6], rowst[:, 4:5], AF.Ln, bias=1e-5, scale=1.0))
        p.fence("scalar")
        SC(lambda e: e.activation(rowst[:, 4:5], rowst[:, 5:6], AF.Exp, bias=0.0, scale=-0.5))
        p.fence("scalar")
        SC(lambda e: e.activation(rowst[:, 6:7], rowst[:, 3:4], AF.Identity, bias=0.0, scale=rowst[:, 4:5]))
        SC(lambda e: e.activation(rowst[:, 6:7], rowst[:, 6:7], AF.Identity, bias=0.0, scale=-1.0))
        p.fence("scalar")
        SC(lambda e: e.activation(xtm[:, :], xtm[:, :], AF.Identity, bias=rowst[:, 6:7], scale=rowst[:, 4:5]))
        for c in range(8):
            bk = c % 2
            T(lambda e, cc=c, b=bk: e.transpose(pbk[b][:, 0:128], xtm[:, 128 * cc:128 * cc + 128], c_eyef[:, :]))
            if want_f32:
                V(lambda e, cc=c, b=bk: e.tensor_copy(dstf[cc][:, dcol:dcol + 128], pbk[b][:, 0:128]))
                G(lambda e, cc=c: e.tensor_copy(dstb[cc][:, dcol:dcol + 128], dstf[cc][:, dcol:dcol + 128]))
            else:
                V(lambda e, cc=c, b=bk: e.tensor_copy(dstb[cc][:, dcol:dcol + 128], pbk[b][:, 0:128]))

    def ffn_attn(l, src, dst, first_layer):
        b0 = 0 if first_layer else 512
        for c in range(8):
            LDMA(stgF[:, 0:1536], rd(wbig[l], 128 * c, 128 * c + 128, 0, 1536))
            G(lambda e, cc=c: e.tensor_copy(w_qkv[cc][:, :], stgF[:, 0:1536]))
        for c in range(8):
            LDMA(stgF[:, 0:1024], rd(wrot_in[l], 128 * c, 128 * c + 128, 0, 1024))
            G(lambda e, cc=c: e.tensor_copy(w_rot[cc][:, :], stgF[:, 0:1024]))
        for c in range(4):
            LDMA(stgF[:, 0:1024], rd(wo_in[l], 128 * c, 128 * c + 128, 0, D))
            G(lambda e, cc=c: e.tensor_copy(w_wo[cc][:, :], stgF[:, 0:1024]))
        LDMA(colA[0:64, 0:1], qsc[l].ap()[:, :])
        LDMA(colA[64:128, 0:1], qsc[l].ap()[:, :])
        LDMA(colA[0:32, 1:2], qsc[l].ap()[32:64, :])
        LDMA(colA[32:64, 1:2], qsc[l].ap()[0:32, :])
        LDMA(colA[64:96, 1:2], qsc[l].ap()[32:64, :])
        LDMA(colA[96:128, 1:2], qsc[l].ap()[0:32, :])
        LDMA(colA[0:64, 2:3], ksc[l].ap()[:, :])
        LDMA(colA[64:128, 2:3], ksc[l].ap()[:, :])
        LDMA(colA[0:32, 3:4], ksc[l].ap()[32:64, :])
        LDMA(colA[32:64, 3:4], ksc[l].ap()[0:32, :])
        LDMA(colA[64:96, 3:4], ksc[l].ap()[32:64, :])
        LDMA(colA[96:128, 3:4], ksc[l].ap()[0:32, :])
        G(lambda e: e.memset(vring[:, :], 1.0))

        ntiles = 7 if first_layer else 6
        for i in range(ntiles):
            tau = 512 * i
            s2 = i % 2
            for c in range(8):
                if i == 0:
                    G(lambda e, cc=c: e.memset(xnb[cc][:, 0:1], 0.0))
                else:
                    G(lambda e, cc=c: e.tensor_copy(xnb[cc][:, 0:1], xnb[cc][:, 512:513]))
            if first_layer:
                for s4 in range(4):
                    ln_group(rd(xk, 512 + tau + 128 * s4, 512 + tau + 128 * s4 + 128, 0, D),
                             xnf, xnb, 1 + 128 * s4, True)
            else:
                for c in range(8):
                    LDMA(xnf[c][:, 1:513], src.ap()[c, :, tau:tau + 512])
                    G(lambda e, cc=c: e.tensor_copy(xnb[cc][:, 1:513], xnf[cc][:, 1:513]))
            for ti in range(4):
                LDMA(tabt[ti][:, :], rd(tabs_in, 0, 128, ti * HBUF + b0 + tau, ti * HBUF + b0 + tau + 512))
                V(lambda e, t2=ti: e.tensor_scalar(tabt[t2][:, :], tabt[t2][:, :], colA[:, t2:t2 + 1], None, OP.mult))

            def rhs_x(c):
                return xnb[c][:, (1 if c < 4 else 0):(513 if c < 4 else 512)]

            def qk(which):
                cb = 0 if which == "q" else 512
                t1_, t2_ = (tabt[0], tabt[1]) if which == "q" else (tabt[2], tabt[3])
                for pp in range(4):
                    for c in range(8):
                        T(lambda e, cc=c, q=pp: e.matmul(pbk[q % 2][:, :], w_qkv[cc][:, cb + 128 * q:cb + 128 * q + 128], rhs_x(cc), start=(cc == 0), stop=(cc == 7)))
                    SC(lambda e, q=pp: e.copy(qA[q][:, :], pbk[q % 2][:, :]))
                    V(lambda e, q=pp: e.tensor_tensor(sqb[:, :], qA[q][:, :], qA[q][:, :], OP.mult))
                    T(lambda e, q=pp: e.matmul(pbk[2][0:8, :], c_oneh[:, 8 * q:8 * q + 8], sqb[:, :], start=(q == 0), stop=(q == 3)))
                SC(lambda e: e.activation(r8a[:, :], pbk[2][0:8, :], AF.Ln, bias=1e-24, scale=1.0))
                SC(lambda e: e.activation(r8a[:, :], r8a[:, :], AF.Exp, bias=0.0, scale=-0.5))
                for pp in range(4):
                    for c in range(8):
                        T(lambda e, cc=c, q=pp: e.matmul(pbk[4 + q][:, :], w_rot[cc][:, cb + 128 * q:cb + 128 * q + 128], rhs_x(cc), start=(cc == 0), stop=(cc == 7)))
                for pp in range(4):
                    T(lambda e, q=pp: e.matmul(pbk[3][:, :], c_bc8[:, 128 * q:128 * q + 128], r8a[:, :], start=True, stop=True))
                    V(lambda e, q=pp, ta=t1_: e.tensor_tensor(sqb[:, :], qA[q][:, :], ta[:, :], OP.mult))
                    V(lambda e, q=pp, tb=t2_: e.tensor_tensor(tmpb[:, :], pbk[4 + q][:, :], tb[:, :], OP.mult))
                    V(lambda e: e.tensor_add(sqb[:, :], sqb[:, :], tmpb[:, :]))
                    if which == "q":
                        V(lambda e, q=pp: e.tensor_tensor(qbuf[q][:, :], sqb[:, :], pbk[3][:, :], OP.mult))
                    else:
                        V(lambda e, q=pp, sl=s2: e.tensor_tensor(kring[q][:, 512 * sl:512 * sl + 512], sqb[:, :], pbk[3][:, :], OP.mult))

            qk("k")
            do_attn = (not first_layer) or (i >= 1)
            if do_attn:
                qk("q")
            for sub in range(4):
                for c in range(8):
                    off = (1 if c < 4 else 0) + 128 * sub
                    for pp in range(4):
                        T(lambda e, cc=c, s=sub, q=pp, o=off: e.matmul(pbk[4 + q][:, 128 * s:128 * s + 128], xnb[cc][:, o:o + 128], w_qkv[cc][:, 1024 + 128 * q:1024 + 128 * q + 128], start=(cc == 0), stop=(cc == 7)))
            for pp in range(4):
                dst520 = vring[:, 520 * s2:520 * s2 + 520] if False else None
                V(lambda e, q=pp, s=s2: e.tensor_copy(
                    kring[q][:, :][:, 0:0] if False else
                    vring[:, 520 * (2 * q + s):520 * (2 * q + s) + 520].rearrange("p (s4 h d) -> p s4 h d", s4=4, h=2, d=65)[:, :, :, 0:64],
                    pbk[4 + q][:, :].rearrange("p (s4 h d) -> p s4 h d", s4=4, h=2, d=64)))
            if (not first_layer):
                LDMA(c_vm[:, :], vmask.ap()[:, tau:tau + 512])
            if do_attn:
                clo = max(0, 4 * i - 4)
                chunks = list(range(clo, 4 * i + 4))
                for hh in range(8):
                    pp = hh // 2
                    half = hh % 2
                    h0 = 64 * half
                    for j0 in range(0, len(chunks), 4):
                        grp = chunks[j0:j0 + 4]
                        for j, ck in enumerate(grp):
                            T(lambda e, q=pp, hz=h0, sl=(ck // 4) % 2, k2=ck % 4, b=j:
                              e.matmul(pbk[b][:, :], kring[q][hz:hz + 64, 512 * sl + 128 * k2:512 * sl + 128 * k2 + 128], qbuf[q][hz:hz + 64, :], start=True, stop=True))
                        for j, ck in enumerate(grp):
                            SC(lambda e, b=j, bc=ck + b0 // 128: e.activation(ptile[b][:, :], pbk[b][:, :], AF.Exp, bias=c_fb[:, bc:bc + 1], scale=8.0))
                        for j, ck in enumerate(grp):
                            if ck >= 4 * i:
                                G(lambda e, b=j, bs=tau - 128 * ck: e.affine_select(out=ptile[b][:, :], in_=ptile[b][:, :], pattern=[[1, 512]], compare_op=OP.is_ge, fill=0.0, base=bs, channel_multiplier=-1))
                            else:
                                G(lambda e, b=j, bs=128 * ck - tau + 512: e.affine_select(out=ptile[b][:, :], in_=ptile[b][:, :], pattern=[[-1, 512]], compare_op=OP.is_ge, fill=0.0, base=bs, channel_multiplier=1))
                        for j, ck in enumerate(grp):
                            T(lambda e, q=pp, sl=(ck // 4) % 2, k2=ck % 4, h=half, b=j, st=(j0 == 0 and j == 0), sp=(j0 + j == len(chunks) - 1):
                              e.matmul(pbk[7][0:65, :], vring[:, 520 * (2 * q + sl) + 130 * k2 + 65 * h:520 * (2 * q + sl) + 130 * k2 + 65 * h + 65], ptile[b][:, :], start=st, stop=sp))
                    V(lambda e: e.tensor_copy(r_a[:, :], pbk[7][64:65, :]))
                    SC(lambda e: e.activation(r_b[:, :], r_a[:, :], AF.Ln, bias=1e-30, scale=1.0))
                    SC(lambda e: e.activation(r_b[:, :], r_b[:, :], AF.Exp, bias=0.0, scale=-1.0))
                    T(lambda e: e.matmul(pbk[6][0:64, :], c_1x[0:1, 0:64], r_b[:, :], start=True, stop=True))
                    V(lambda e: e.tensor_copy(rbsb[:, :], pbk[6][0:64, :]))
                    V(lambda e, c2=pp, h=half: e.tensor_tensor(ofb[c2][64 * h:64 * h + 64, :], pbk[7][0:64, :], rbsb[:, :], OP.mult))
                for c in range(8):
                    bk = c % 2
                    for ic in range(4):
                        T(lambda e, c2=ic, cc=c, b=bk: e.matmul(pbk[b][:, :], w_wo[c2][:, 128 * cc:128 * cc + 128], ofb[c2][:, :], start=(c2 == 0), stop=(c2 == 3)))
                    rs = c % 2
                    V(lambda e, cc=c, b=bk, r2=rs: e.tensor_add(resb[r2][:, :], pbk[b][:, :], xnf[cc][:, 1:513]))
                    if not first_layer:
                        V(lambda e, r2=rs: e.tensor_tensor(resb[r2][:, :], resb[r2][:, :], c_vm[:, :], OP.mult))
                    SDMA(dst.ap()[c, :, tau - (512 if first_layer else 0):tau - (512 if first_layer else 0) + 512], resb[rs][:, :])

    def mlp(l, srcscr, hnscr, prtscr, dstscr, jt, scol0, mask_dst):
        LDMA(lg_s[:, :], lgrow[l].ap()[:, :])
        LDMA(lb_s[:, :], lbrow[l].ap()[:, :])
        LDMA(b2_s[:, :], b2cin[l].ap()[:, :])
        LDMA(b1_s[:, :], b1c[l].ap()[:, :])
        for j in range(jt):
            sc = scol0 + 512 * j
            for c in range(8):
                LDMA(xnf[c][:, 1:513], srcscr.ap()[c, :, sc:sc + 512])
                if sc == 0:
                    G(lambda e, cc=c: e.memset(xnf[cc][:, 0:1], 0.0))
                else:
                    LDMA(xnf[c][:, 0:1], srcscr.ap()[c, :, sc - 1:sc], slow=True)

            def hsl(c):
                return xnf[c][:, (1 if c < 4 else 0):(513 if c < 4 else 512)]

            for c in range(8):
                V(lambda e, cc=c: e.tensor_tensor(sqb[:, :], hsl(cc), hsl(cc), OP.mult))
                T(lambda e, cc=c: e.matmul(pbk[2][0:1, :], c_1col[:, :], hsl(cc), start=(cc == 0), stop=(cc == 7)))
                T(lambda e, cc=c: e.matmul(pbk[5][0:1, :], c_1col[:, :], sqb[:, :], start=(cc == 0), stop=(cc == 7)))
            V(lambda e: e.tensor_scalar(r_m[:, :], pbk[2][0:1, :], 1.0 / D, None, OP.mult))
            V(lambda e: e.tensor_tensor(r_a[:, :], r_m[:, :], r_m[:, :], OP.mult))
            V(lambda e: e.scalar_tensor_tensor(r_a[:, :], pbk[5][0:1, :], 1.0 / D, r_a[:, :], OP.mult, OP.subtract))
            SC(lambda e: e.activation(r_b[:, :], r_a[:, :], AF.Ln, bias=1e-5, scale=1.0))
            SC(lambda e: e.activation(r_b[:, :], r_b[:, :], AF.Exp, bias=0.0, scale=-0.5))
            V(lambda e: e.scalar_tensor_tensor(r_a[:, :], r_m[:, :], -1.0, r_b[:, :], OP.mult, OP.mult))
            for c in range(8):
                T(lambda e, cc=c: e.matmul(pbk[3][:, :], lg_s[0:1, 128 * cc:128 * cc + 128], r_b[:, :], start=True, stop=True))
                T(lambda e, cc=c: e.matmul(pbk[4][:, :], lg_s[0:1, 128 * cc:128 * cc + 128], r_a[:, :], start=True, stop=False))
                T(lambda e, cc=c: e.matmul(pbk[4][:, :], lb_s[0:1, 128 * cc:128 * cc + 128], c_orow[:, :], start=False, stop=True))
                V(lambda e, cc=c: e.tensor_tensor(tmpb[:, :], hsl(cc), pbk[3][:, :], OP.mult))
                V(lambda e: e.tensor_add(sqb[:, :], tmpb[:, :], pbk[4][:, :]))
                G(lambda e, cc=c: e.tensor_copy(hnb[cc][:, :], sqb[:, :]))
                SDMA(hnscr.ap()[c, :, 512 * j:512 * j + 512], hnb[c][:, :])
        for half in range(2):
            for c in range(8):
                LDMA(stgF[:, 0:2816], rd(w1p[l], 128 * c, 128 * c + 128, 2816 * half, 2816 * half + 2816))
                G(lambda e, cc=c: e.tensor_copy(w1h[cc][:, :], stgF[:, 0:2816]))
            for mc in range(11):
                LDMA(stgF[:, 0:1024], rd(w2p[l], 128 * (11 * half + mc), 128 * (11 * half + mc) + 128, 0, D))
                G(lambda e, m=mc: e.tensor_copy(w2h[:, 1024 * m:1024 * m + 1024], stgF[:, 0:1024]))
            for j in range(jt):
                sc = scol0 + 512 * j
                if half == 1 and mask_dst:
                    LDMA(c_vm[:, :], vmask.ap()[:, 512 * j:512 * j + 512])
                for c in range(8):
                    LDMA(hnb[c][:, :], hnscr.ap()[c, :, 512 * j:512 * j + 512])
                for mc in range(11):
                    ma = 2 * (11 * half + mc)
                    for c in range(8):
                        T(lambda e, cc=c, m=mc: e.matmul(pbk[0][:, :], w1h[cc][:, 256 * m:256 * m + 128], hnb[cc][:, :], start=(cc == 0), stop=(cc == 7)))
                    for c in range(8):
                        T(lambda e, cc=c, m=mc: e.matmul(pbk[1][:, :], w1h[cc][:, 256 * m + 128:256 * m + 256], hnb[cc][:, :], start=(cc == 0), stop=(cc == 7)))
                    SC(lambda e, m2=ma: e.activation(gelb[:, :], pbk[1][:, :], AF.Gelu, bias=b1_s[:, m2 + 1:m2 + 2], scale=1.0))
                    V(lambda e, m=mc, m2=ma: e.scalar_tensor_tensor(prodb[m][:, :], pbk[0][:, :], b1_s[:, m2:m2 + 1], gelb[:, :], OP.add, OP.mult))
                for c in range(8):
                    bk = 4 + (c % 4)
                    for mc2 in range(11):
                        T(lambda e, m=mc2, cc=c, b=bk: e.matmul(pbk[b][:, :], w2h[:, 1024 * m + 128 * cc:1024 * m + 128 * cc + 128], prodb[m][:, :], start=(m == 0), stop=(m == 10)))
                    ss = c % 2
                    if half == 0:
                        V(lambda e, b=bk, s=ss: e.tensor_copy(resb[s][:, :], pbk[b][:, :]))
                        SDMA(prtscr.ap()[c, :, 512 * j:512 * j + 512], resb[ss][:, :])
                    else:
                        LDMA(tmpb[:, :], prtscr.ap()[c, :, 512 * j:512 * j + 512])
                        LDMA(sqb[:, :], srcscr.ap()[c, :, sc:sc + 512])
                        V(lambda e, b=bk, s=ss: e.tensor_add(resb[s][:, :], pbk[b][:, :], tmpb[:, :]))
                        V(lambda e, s=ss: e.tensor_add(resb[s][:, :], resb[s][:, :], sqb[:, :]))
                        V(lambda e, s=ss, cc=c: e.tensor_scalar(resb[s][:, :], resb[s][:, :], b2_s[:, cc:cc + 1], None, OP.add))
                        if mask_dst:
                            V(lambda e, s=ss: e.tensor_tensor(resb[s][:, :], resb[s][:, :], c_vm[:, :], OP.mult))
                        SDMA(dstscr.ap()[c, :, 512 * j:512 * j + 512], resb[ss][:, :])

    def attn_phase():
        G(lambda e: e.memset(vall[:, :], 1.0))
        for ph in range(2):
            for c in range(8):
                LDMA(stgF[:, 0:512], rd(wv_att, 128 * c, 128 * c + 128, 512 * ph, 512 * ph + 512))
                G(lambda e, cc=c: e.tensor_copy(wv_h[cc][:, :], stgF[:, 0:512]))
            for tg in range(8):
                if ph == 0:
                    for s4 in range(4):
                        ln_group(rd(xk, 512 * tg + 128 * s4, 512 * tg + 128 * s4 + 128, 0, D), None, xnt8, 128 * s4, False)
                    for c in range(8):
                        SDMA(xns.ap()[c, :, 512 * tg:512 * tg + 512], xnt8[c][:, :])
                else:
                    for c in range(8):
                        LDMA(xnt8[c][:, :], xns.ap()[c, :, 512 * tg:512 * tg + 512])
                for sub in range(4):
                    for c in range(8):
                        T(lambda e, cc=c, s=sub: e.matmul(pbk[4][:, :], xnt8[cc][:, 128 * s:128 * s + 128], wv_h[cc][:, :], start=(cc == 0), stop=(cc == 7)))
                    tc = 4 * tg + sub
                    V(lambda e, t2=tc, p2=ph: e.tensor_copy(
                        vall[:, 1040 * t2 + 520 * p2:1040 * t2 + 520 * p2 + 520].rearrange("p (q h d) -> p q h d", q=4, h=2, d=65)[:, :, :, 0:64],
                        pbk[4][:, :].rearrange("p (q h d) -> p q h d", q=4, h=2, d=64)))
        for pp in range(8):
            for c in range(8):
                LDMA(stgF[:, 0:128], rd(wq_att, 128 * c, 128 * c + 128, 128 * pp, 128 * pp + 128))
                LDMA(stgF[:, 128:256], rd(wk_att, 128 * c, 128 * c + 128, 128 * pp, 128 * pp + 128))
                G(lambda e, cc=c: e.tensor_copy(wqk_a[cc][:, :], stgF[:, 0:256]))
            for kt in range(8):
                for c in range(8):
                    LDMA(xnt8[c][:, :], xns.ap()[c, :, 512 * kt:512 * kt + 512])
                for c in range(8):
                    T(lambda e, cc=c: e.matmul(pbk[0][:, :], wqk_a[cc][:, 128:256], xnt8[cc][:, :], start=(cc == 0), stop=(cc == 7)))
                SC(lambda e, k2=kt: e.copy(kpr[:, 512 * k2:512 * k2 + 512], pbk[0][:, :]))
                if kt >= 4:
                    for c in range(8):
                        T(lambda e, cc=c: e.matmul(pbk[1][:, :], wqk_a[cc][:, 0:128], xnt8[cc][:, :], start=(cc == 0), stop=(cc == 7)))
                    SC(lambda e, k2=kt - 4: e.copy(qpr[:, 512 * k2:512 * k2 + 512], pbk[1][:, :]))
            for hh in range(2):
                h0 = 64 * hh
                for qt in range(4):
                    kmax = 20 + 4 * qt
                    for j0 in range(0, kmax, 4):
                        grp = list(range(j0, min(j0 + 4, kmax)))
                        for j, ck in enumerate(grp):
                            T(lambda e, k2=ck, b=j, h=h0, q=qt: e.matmul(pbk[b][:, :], kpr[h:h + 64, 128 * k2:128 * k2 + 128], qpr[h:h + 64, 512 * q:512 * q + 512], start=True, stop=True))
                        for j, ck in enumerate(grp):
                            SC(lambda e, b=j, k2=ck: e.activation(aptile[b][:, :], pbk[b][:, :], AF.Exp, bias=c_kb[:, k2:k2 + 1], scale=0.125))
                        for j, ck in enumerate(grp):
                            if ck >= kmax - 4:
                                G(lambda e, b=j, bs=2048 + 512 * qt - 128 * ck: e.affine_select(out=aptile[b][:, :], in_=aptile[b][:, :], pattern=[[1, 512]], compare_op=OP.is_ge, fill=0.0, base=bs, channel_multiplier=-1))
                        for j, ck in enumerate(grp):
                            T(lambda e, k2=ck, h=hh, b=j, q=pp, st=(j0 == 0 and j == 0), sp=(j0 + j == kmax - 1):
                              e.matmul(pbk[7][0:65, :], vall[:, 1040 * k2 + 520 * (q // 4) + 130 * (q % 4) + 65 * h:1040 * k2 + 520 * (q // 4) + 130 * (q % 4) + 65 * h + 65], aptile[b][:, :], start=st, stop=sp))
                    V(lambda e: e.tensor_copy(r_a[:, :], pbk[7][64:65, :]))
                    SC(lambda e: e.activation(r_b[:, :], r_a[:, :], AF.Ln, bias=1e-30, scale=1.0))
                    SC(lambda e: e.activation(r_b[:, :], r_b[:, :], AF.Exp, bias=0.0, scale=-1.0))
                    T(lambda e: e.matmul(pbk[6][0:64, :], c_1x[0:1, 0:64], r_b[:, :], start=True, stop=True))
                    V(lambda e: e.tensor_copy(rbsb[:, :], pbk[6][0:64, :]))
                    V(lambda e, h=h0: e.tensor_tensor(osm[h:h + 64, :], pbk[7][0:64, :], rbsb[:, :], OP.mult))
                    SDMA(oalls.ap()[pp, 64 * hh:64 * hh + 64, 512 * qt:512 * qt + 512], osm[64 * hh:64 * hh + 64, :])
        for qt in range(4):
            for ic in range(8):
                LDMA(oqt[ic][:, :], oalls.ap()[ic, :, 512 * qt:512 * qt + 512])
            for wh in range(2):
                for c2 in range(4):
                    cc = 4 * wh + c2
                    LDMA(stgF[:, 0:1024], rd(wcat_in, 128 * cc, 128 * cc + 128, 0, D))
                    G(lambda e, c3=c2: e.tensor_copy(wcat_s[c3][:, :], stgF[:, 0:1024]))
                for c in range(8):
                    for ic in range(4):
                        T(lambda e, c2=ic, cc=c, b=c, w=wh: e.matmul(pbk[b][:, :], wcat_s[c2][:, 128 * cc:128 * cc + 128], oqt[4 * w + c2][:, :], start=(c2 == 0 and w == 0), stop=(c2 == 3 and w == 1)))
                if wh == 0:
                    continue
                for c in range(8):
                    LDMA(tmpb[:, :], Fs.ap()[c, :, 512 * qt:512 * qt + 512])
                    V(lambda e, b=c, cc=c: e.scalar_tensor_tensor(xnf[cc][:, 0:512], pbk[b][:, :], bo_s[:, cc:cc + 1], tmpb[:, :], OP.add, OP.add))
            for s4 in range(4):
                for dmg in range(2):
                    for c2 in range(4):
                        cc = 4 * dmg + c2
                        T(lambda e, c3=cc, s=s4, b=2 + dmg, c4=c2: e.transpose(pbk[b][:, 128 * c4:128 * c4 + 128], xnf[c3][:, 128 * s:128 * s + 128], c_eyef[:, :]))
                LDMA(xtm[:, :], rd(xk, 2048 + 512 * qt + 128 * s4, 2048 + 512 * qt + 128 * s4 + 128, 0, D))
                V(lambda e: e.tensor_add(resb[0][:, :], pbk[2][:, :], xtm[:, 0:512]))
                V(lambda e: e.tensor_add(resb[1][:, :], pbk[3][:, :], xtm[:, 512:1024]))
                SDMA(out.ap()[512 * qt + 128 * s4:512 * qt + 128 * s4 + 128, 0:512], resb[0][:, :])
                SDMA(out.ap()[512 * qt + 128 * s4:512 * qt + 128 * s4 + 128, 512:1024], resb[1][:, :])

    wv_h = [wv_a[i // 2][:, 512 * (i % 2):512 * (i % 2) + 512] for i in range(8)]

    ffn_attn(0, None, a1s, True)
    mlp(0, a1s, hn1, prt, x2s, 6, 0, True)
    ffn_attn(1, x2s, a2s, False)
    mlp(1, a2s, hn2, prt2, Fs, 4, 1024, False)
    attn_phase()
    p.step("sync", None)

    import contextlib
    stack = contextlib.ExitStack()
    sems = {}
    for s in ("T", "V", "S", "G", "Y", "LD", "ST"):
        sems[s] = stack.enter_context(nc.semaphore(f"sem{s}"))
    blk = stack.enter_context(nc.Block())

    def mk(engname):
        def body(eng):
            for fn, waits, sem, amt in p.streams[engname]:
                for (s, v) in waits:
                    eng.wait_ge(sems[s], v)
                if fn is not None:
                    fn(eng).then_inc(sems[sem], amt)
        return body

    blk.tensor(mk("tensor"))
    blk.vector(mk("vector"))
    blk.scalar(mk("scalar"))
    blk.gpsimd(mk("gpsimd"))
    blk.sync(mk("sync"))
    stack.close()
    return nc


def _tables(core_h):
    toff = 512 if core_h == 1 else -1536
    t = np.arange(HBUF, dtype=np.float64) + toff
    d = np.arange(64)
    half = d % 32
    inv_freq = 1.0 / (10000.0 ** (2 * half / 64.0))
    freqs = t[:, None] * inv_freq[None, :]
    base = (2 * half + 0.4 * 64) / (1.4 * 64)
    power = (t - 2048.0) / 256.0
    xsc = base[None, :] ** power[:, None]
    sgn = np.where(d < 32, -1.0, 1.0)
    t1q = np.cos(freqs) * xsc
    t2q = np.sin(freqs) * xsc * sgn[None, :]
    t1k = np.cos(freqs) / xsc
    t2k = np.sin(freqs) / xsc * sgn[None, :]
    tab = np.concatenate([t1q.T, t2q.T, t1k.T, t2k.T], axis=1).astype(np.float32)
    return np.ascontiguousarray(np.concatenate([tab, tab], axis=0))


_NC_CACHE = {}
_LAST_RES = None


def kernel(L0_attn_resid, L0_attn_W_Q, L0_attn_W_K, L0_attn_W_V, L0_attn_W_out,
           L0_attn_b_out, L0_ffn_wqkv, L0_ffn_q_scale, L0_ffn_k_scale, L0_ffn_wo,
           L0_ffn_ln_g, L0_ffn_ln_b, L0_ffn_w1, L0_ffn_b1, L0_ffn_w2, L0_ffn_b2):
    x = np.ascontiguousarray(np.asarray(L0_attn_resid, np.float32))
    f32 = lambda a: np.ascontiguousarray(np.asarray(a, np.float32))
    sh = {}
    sh["eyef"] = np.eye(128, dtype=np.float32)
    oneh = np.zeros((128, 32), np.float32)
    for pp in range(4):
        oneh[0:64, 8 * pp + 2 * pp] = 1.0
        oneh[64:128, 8 * pp + 2 * pp + 1] = 1.0
    sh["onehots"] = oneh
    sh["onescol"] = np.ones((128, 1), np.float32)
    bc8 = np.zeros((8, 512), np.float32)
    for pp in range(4):
        bc8[2 * pp, 128 * pp:128 * pp + 64] = 1.0
        bc8[2 * pp + 1, 128 * pp + 64:128 * pp + 128] = 1.0
    sh["bcast8"] = bc8
    sh["ones1x"] = np.ones((1, 128), np.float32)
    sh["onesrow"] = np.ones((1, 512), np.float32)
    sh["negio"] = np.tile(-np.arange(512, dtype=np.float32)[None, :], (128, 1))
    sh["constv"] = np.tile(np.array([1e-5, 1e-24, 1e-30, 0.0], np.float32)[None, :], (128, 1))
    wqkv = f32(L0_ffn_wqkv)
    perm = (np.arange(64) + 32) % 64
    for l in range(2):
        sh[f"wbig{l}"] = np.ascontiguousarray(wqkv[l])
        rot = np.zeros((D, 1024), np.float32)
        rot[:, 0:512] = wqkv[l][:, 0:512].reshape(D, 8, 64)[:, :, perm].reshape(D, 512)
        rot[:, 512:1024] = wqkv[l][:, 512:1024].reshape(D, 8, 64)[:, :, perm].reshape(D, 512)
        sh[f"wrot{l}"] = rot
        sh[f"wo{l}"] = f32(L0_ffn_wo)[l]
        w1 = f32(L0_ffn_w1)[l]
        b1 = f32(L0_ffn_b1)[l]
        a = np.zeros((D, MP * 128), np.float32)
        g = np.zeros((D, MP * 128), np.float32)
        a[:, 0:2730] = w1[:, 0:2730]
        g[:, 0:2730] = w1[:, 2730:5460]
        ba = np.zeros(MP * 128, np.float32)
        bg = np.zeros(MP * 128, np.float32)
        ba[0:2730] = b1[0:2730]
        bg[0:2730] = b1[2730:5460]
        w1pp = np.zeros((D, MP * 256), np.float32)
        b1cc = np.zeros((128, 2 * MP), np.float32)
        for m in range(MP):
            w1pp[:, 256 * m:256 * m + 128] = a[:, 128 * m:128 * m + 128]
            w1pp[:, 256 * m + 128:256 * m + 256] = g[:, 128 * m:128 * m + 128]
            b1cc[:, 2 * m] = ba[128 * m:128 * m + 128]
            b1cc[:, 2 * m + 1] = bg[128 * m:128 * m + 128]
        sh[f"w1p{l}"] = w1pp
        sh[f"b1c{l}"] = b1cc
        w2pp = np.zeros((MP * 128, D), np.float32)
        w2pp[0:2730, :] = f32(L0_ffn_w2)[l]
        sh[f"w2p{l}"] = w2pp
        sh[f"b2cin{l}"] = np.ascontiguousarray(f32(L0_ffn_b2)[l].reshape(8, 128).T)
        sh[f"lgrow{l}"] = f32(L0_ffn_ln_g)[l][None, :]
        sh[f"lbrow{l}"] = f32(L0_ffn_ln_b)[l][None, :]
        sh[f"qsc{l}"] = f32(L0_ffn_q_scale)[l][:, None]
        sh[f"ksc{l}"] = f32(L0_ffn_k_scale)[l][:, None]
    sh["wq_att"] = f32(L0_attn_W_Q).reshape(D, D)
    sh["wk_att"] = f32(L0_attn_W_K).reshape(D, D)
    sh["wv_att"] = f32(L0_attn_W_V).reshape(D, D)
    sh["wcat"] = np.ascontiguousarray(f32(L0_attn_W_out).transpose(1, 0, 2).reshape(D, D))
    sh["bocin"] = np.ascontiguousarray(f32(L0_attn_b_out).reshape(8, 128).T)

    tab_lo, tab_hi = _tables(0), _tables(1)
    kb_lo = np.full((128, 32), -24.0, np.float32)
    kb_lo[:, 0:16] = -1e9
    kb_hi = np.full((128, 32), -24.0, np.float32)
    fb_lo = np.full((128, 28), -32.0, np.float32)
    fb_lo[:, 0:12] = -1e9
    fb_hi = np.full((128, 28), -32.0, np.float32)
    vl_lo = np.full((128, 1), -1535.5, np.float32)
    vl_hi = np.full((128, 1), 1e9, np.float32)

    if "nc" not in _NC_CACHE:
        _NC_CACHE["nc"] = build_nc()
    nc = _NC_CACHE["nc"]
    in_maps = []
    for core in range(8):
        b, h = core // 2, core % 2
        xkc = x[b] if h == 1 else np.concatenate([np.zeros((2048, D), np.float32), x[b, 0:2048]], axis=0)
        vm = np.ones((128, 3072), np.float32)
        if h == 0:
            vm[:, 0:1024] = 0.0
        m = {"vmask": vm,
             "xk": np.ascontiguousarray(xkc),
             "kbias": kb_hi if h == 1 else kb_lo,
             "fbias": fb_hi if h == 1 else fb_lo,
             "tabs": tab_hi if h == 1 else tab_lo,
             "vlen": vl_hi if h == 1 else vl_lo}
        m.update(sh)
        in_maps.append(m)
    res = run_bass_kernel_spmd(nc, in_maps, core_ids=list(range(8)))
    global _LAST_RES
    _LAST_RES = res
    o = np.zeros((B, S, D), np.float32)
    for core in range(8):
        b, h = core // 2, core % 2
        o[b, 2048 * h:2048 * h + 2048] = res.results[core]["out"]
    return o


# version 07
# speedup vs baseline: 1.1498x; 1.0154x over previous
import numpy as np
import concourse.bass as bass
import concourse.mybir as mybir
from concourse.bass_utils import run_bass_kernel_spmd

F32 = mybir.dt.float32
BF16 = mybir.dt.bfloat16
AF = mybir.ActivationFunctionType
OP = mybir.AluOpType
AX = mybir.AxisListType

B, S, D = 4, 4096, 1024
HBUF = 3584
MP = 22
ENGINES = ("tensor", "vector", "scalar", "gpsimd", "sync")
ESEM = {"tensor": "T", "vector": "V", "scalar": "S", "gpsimd": "G", "sync": "Y"}


class Prog:
    def __init__(self):
        self.streams = {e: [] for e in ENGINES}
        self.cnt = {k: 0 for k in ("T", "V", "S", "G", "Y", "LD", "ST")}
        self.hw = {}
        self.last = None

    def step(self, eng, fn, sem=None, amt=1):
        sem = sem or ESEM[eng]
        waits = []
        t = self.last
        if t is not None:
            s, v = t
            if not (s == ESEM[eng]) and not (sem in ("LD", "ST") and s == sem):
                k = (eng, s)
                if self.hw.get(k, 0) < v:
                    self.hw[k] = v
                    waits.append((s, v))
        self.cnt[sem] += amt
        tok = (sem, self.cnt[sem])
        self.streams[eng].append((fn, waits, sem, amt))
        self.last = tok
        return tok
    def fence(self, eng):
        sem = ESEM[eng]
        v = self.cnt[sem]
        if v > 0:
            self.streams[eng].append((None, [(sem, v)], sem, 0))



def build_nc(debug=False):
    nc = bass.Bass("TRN2", target_bir_lowering=False)
    p = Prog()

    def din(name, shape, dt=F32):
        return nc.dram_tensor(name, shape, dt, kind="ExternalInput")

    xk = din("xk", [S, D])
    kbias = din("kbias", [128, 32])
    fbias = din("fbias", [128, 28])
    tabs_in = din("tabs", [128, 4 * HBUF])
    vlen = din("vlen", [128, 1])
    eyef = din("eyef", [128, 128])
    onehots = din("onehots", [128, 32])
    onescol = din("onescol", [128, 1])
    bcast8 = din("bcast8", [8, 512])
    ones1x = din("ones1x", [1, 128])
    onesrow = din("onesrow", [1, 512])
    negio = din("negio", [128, 512])
    constv = din("constv", [128, 4])
    vmask = din("vmask", [128, 3072])
    wbig = [din(f"wbig{l}", [D, 1536]) for l in range(2)]
    wrot_in = [din(f"wrot{l}", [D, 1024]) for l in range(2)]
    wo_in = [din(f"wo{l}", [512, D]) for l in range(2)]
    w1p = [din(f"w1p{l}", [D, MP * 256]) for l in range(2)]
    w2p = [din(f"w2p{l}", [MP * 128, D]) for l in range(2)]
    b1c = [din(f"b1c{l}", [128, 2 * MP]) for l in range(2)]
    b2cin = [din(f"b2cin{l}", [128, 8]) for l in range(2)]
    lgrow = [din(f"lgrow{l}", [1, D]) for l in range(2)]
    lbrow = [din(f"lbrow{l}", [1, D]) for l in range(2)]
    qsc = [din(f"qsc{l}", [64, 1]) for l in range(2)]
    ksc = [din(f"ksc{l}", [64, 1]) for l in range(2)]
    wq_att = din("wq_att", [D, D])
    wk_att = din("wk_att", [D, D])
    wv_att = din("wv_att", [D, D])
    wcat_in = din("wcat", [D, D])
    bocin = din("bocin", [128, 8])
    out = nc.dram_tensor("out", [2048, D], F32, kind="ExternalOutput")

    kd = dict(kind="ExternalOutput") if debug else {}
    a1s = nc.dram_tensor("a1s", [8, 128, 3072], F32, **kd)
    hn1 = nc.dram_tensor("hn1", [8, 128, 3072], BF16, **kd)
    x2s = nc.dram_tensor("x2s", [8, 128, 3072], F32, **kd)
    prt = nc.dram_tensor("prt", [8, 128, 3072], F32)
    a2s = nc.dram_tensor("a2s", [8, 128, 3072], F32, **kd)
    hn2 = nc.dram_tensor("hn2", [8, 128, 2048], BF16)
    prt2 = nc.dram_tensor("prt2", [8, 128, 2048], F32)
    Fs = nc.dram_tensor("Fs", [8, 128, 2048], F32, **kd)
    xns = nc.dram_tensor("xns", [8, 128, 4096], BF16, **kd)
    oalls = nc.dram_tensor("oalls", [8, 128, 2048], BF16, **kd)

    AB = nc.alloc_sbuf_tensor("AB", [128, 57472], BF16)
    AFa = nc.alloc_sbuf_tensor("AFa", [128, 11784], F32)
    c_eyef = nc.alloc_sbuf_tensor("c_eyef", [128, 128], F32)
    c_oneh = nc.alloc_sbuf_tensor("c_oneh", [128, 32], F32)
    c_1col = nc.alloc_sbuf_tensor("c_1col", [128, 1], F32)
    c_bc8 = nc.alloc_sbuf_tensor("c_bc8", [8, 512], F32)
    c_1x = nc.alloc_sbuf_tensor("c_1x", [1, 128], F32)
    c_orow = nc.alloc_sbuf_tensor("c_orow", [1, 512], F32)
    c_kb = nc.alloc_sbuf_tensor("c_kb", [128, 32], F32)
    c_fb = nc.alloc_sbuf_tensor("c_fb", [128, 28], F32)
    c_ni = nc.alloc_sbuf_tensor("c_ni", [128, 512], F32)
    c_vl = nc.alloc_sbuf_tensor("c_vl", [128, 1], F32)
    rowst = nc.alloc_sbuf_tensor("rowst", [128, 8], F32)
    r_m = nc.alloc_sbuf_tensor("r_m", [1, 512], F32)
    r_a = nc.alloc_sbuf_tensor("r_a", [1, 512], F32)
    r_b = nc.alloc_sbuf_tensor("r_b", [1, 512], F32)
    r8a = nc.alloc_sbuf_tensor("r8a", [8, 512], F32)
    colA = nc.alloc_sbuf_tensor("colA", [128, 4], F32)
    lg_s = nc.alloc_sbuf_tensor("lg_s", [1, D], F32)
    lb_s = nc.alloc_sbuf_tensor("lb_s", [1, D], F32)
    b2_s = nc.alloc_sbuf_tensor("b2_s", [128, 8], F32)
    b1_s = nc.alloc_sbuf_tensor("b1_s", [128, 2 * MP], F32)
    bo_s = nc.alloc_sbuf_tensor("bo_s", [128, 8], F32)
    c_cv = nc.alloc_sbuf_tensor("c_cv", [128, 4], F32)
    c_vm = nc.alloc_sbuf_tensor("c_vm", [128, 512], F32)
    for _i, _v in enumerate([1e-5, 1e-24, 1e-30]):
        nc.const_aps.register(F32, _v, c_cv[:, _i:_i + 1])

    pbk = [nc.alloc_psum_tensor(f"pbk{i}", [128, 512], F32) for i in range(8)]

    class Arena:
        def __init__(self, t, n):
            self.t, self.n, self.off = t, n, 0

        def take(self, cols):
            o = self.off
            self.off += cols
            assert self.off <= self.n, f"arena overflow {self.off}>{self.n}"
            return self.t[:, o:o + cols]

        def reset(self):
            self.off = 0

    ab = Arena(AB, 57472)
    af = Arena(AFa, 11784)

    xnf = [af.take(513) for _ in range(8)]
    sqb = af.take(512)
    tmpb = af.take(512)
    rbsb_full = af.take(512)
    rbsb = rbsb_full[0:64, :]
    resb = [af.take(512) for _ in range(2)]
    xtm = af.take(1024)
    _shared = af.take(4096)
    qA = [_shared[:, 512 * i:512 * i + 512] for i in range(4)]
    tabt = [_shared[:, 2048 + 512 * i:2048 + 512 * i + 512] for i in range(4)]
    stgF = _shared[:, 0:4096]

    ab.reset()
    w_qkv = [ab.take(1536) for _ in range(8)]
    w_rot = [ab.take(1024) for _ in range(8)]
    w_wo = [ab.take(1024) for _ in range(4)]
    xnb = [ab.take(513) for _ in range(8)]
    kring = [ab.take(1024) for _ in range(4)]
    vring = ab.take(4160)
    qbuf = [ab.take(512) for _ in range(4)]
    ptile = [ab.take(512) for _ in range(4)]
    ofb = [ab.take(512) for _ in range(4)]
    ab.reset()
    w1h = [ab.take(2816) for _ in range(8)]
    w2h = ab.take(11264)
    hnb = [ab.take(512) for _ in range(8)]
    prodb = [ab.take(512) for _ in range(11)]
    gelb = ab.take(512)
    gel2 = ab.take(512)
    ab.reset()
    vall = ab.take(33280)
    xnt8 = [ab.take(512) for _ in range(8)]
    kpr = ab.take(4096)
    qpr = ab.take(2048)
    wv_a = [ab.take(1024) for _ in range(4)]
    wqk_a = [ab.take(256) for _ in range(8)]
    aptile = [ab.take(512) for _ in range(6)]
    oqt = [ab.take(512) for _ in range(8)]
    osm = ab.take(512)
    wcat_s = wv_a

    def T(fn):
        return p.step("tensor", fn)

    def V(fn):
        return p.step("vector", fn)

    def SC(fn):
        return p.step("scalar", fn)

    def G(fn):
        return p.step("gpsimd", fn)

    def LDMA(dst, src, slow=False):
        return p.step("sync", lambda e, d=dst, s=src, sl=slow: e.dma_start(out=d, in_=s, allow_slow_non_contiguous=sl), "LD", 16)

    def SDMA(dst, src):
        return p.step("gpsimd", lambda e, d=dst, s=src: e.dma_start(out=d, in_=s), "ST", 16)

    def rd(t, r0, r1, c0, c1):
        return t.ap()[r0:r1, c0:c1]

    LDMA(c_eyef[:, :], eyef.ap()[:, :])
    LDMA(c_oneh[:, :], onehots.ap()[:, :])
    LDMA(c_1col[:, :], onescol.ap()[:, :])
    LDMA(c_bc8[:, :], bcast8.ap()[:, :])
    LDMA(c_1x[:, :], ones1x.ap()[:, :])
    LDMA(c_orow[:, :], onesrow.ap()[:, :])
    LDMA(c_kb[:, :], kbias.ap()[:, :])
    LDMA(c_fb[:, :], fbias.ap()[:, :])
    LDMA(c_ni[:, :], negio.ap()[:, :])
    LDMA(c_vl[:, :], vlen.ap()[:, :])
    LDMA(bo_s[:, :], bocin.ap()[:, :])
    LDMA(c_cv[:, :], constv.ap()[:, :])

    def ln_group(src_rows_ap, dstf, dstb, dcol, want_f32):
        LDMA(xtm[:, :], src_rows_ap)
        V(lambda e: e.tensor_reduce(rowst[:, 0:1], xtm[:, :], AX.X, OP.add))
        V(lambda e: e.tensor_tensor(sqb[:, :], xtm[:, 0:512], xtm[:, 0:512], OP.mult))
        V(lambda e: e.tensor_reduce(rowst[:, 1:2], sqb[:, :], AX.X, OP.add))
        V(lambda e: e.tensor_tensor(sqb[:, :], xtm[:, 512:1024], xtm[:, 512:1024], OP.mult))
        V(lambda e: e.tensor_reduce(rowst[:, 2:3], sqb[:, :], AX.X, OP.add))
        p.fence("scalar")
        SC(lambda e: e.activation(rowst[:, 3:4], rowst[:, 0:1], AF.Identity, bias=0.0, scale=1.0 / D))
        p.fence("scalar")
        SC(lambda e: e.activation(rowst[:, 5:6], rowst[:, 1:2], AF.Identity, bias=rowst[:, 2:3], scale=1.0))
        p.fence("scalar")
        SC(lambda e: e.activation(rowst[:, 6:7], rowst[:, 3:4], AF.Square, bias=0.0, scale=1.0))
        p.fence("scalar")
        SC(lambda e: e.activation(rowst[:, 6:7], rowst[:, 6:7], AF.Identity, bias=0.0, scale=-1.0))
        p.fence("scalar")
        SC(lambda e: e.activation(rowst[:, 4:5], rowst[:, 5:6], AF.Identity, bias=rowst[:, 6:7], scale=1.0 / D))
        p.fence("scalar")
        SC(lambda e: e.activation(rowst[:, 5:6], rowst[:, 4:5], AF.Ln, bias=1e-5, scale=1.0))
        p.fence("scalar")
        SC(lambda e: e.activation(rowst[:, 4:5], rowst[:, 5:6], AF.Exp, bias=0.0, scale=-0.5))
        p.fence("scalar")
        SC(lambda e: e.activation(rowst[:, 6:7], rowst[:, 3:4], AF.Identity, bias=0.0, scale=rowst[:, 4:5]))
        SC(lambda e: e.activation(rowst[:, 6:7], rowst[:, 6:7], AF.Identity, bias=0.0, scale=-1.0))
        p.fence("scalar")
        SC(lambda e: e.activation(xtm[:, :], xtm[:, :], AF.Identity, bias=rowst[:, 6:7], scale=rowst[:, 4:5]))
        for c in range(8):
            bk = c % 2
            T(lambda e, cc=c, b=bk: e.transpose(pbk[b][:, 0:128], xtm[:, 128 * cc:128 * cc + 128], c_eyef[:, :]))
            if want_f32:
                V(lambda e, cc=c, b=bk: e.tensor_copy(dstf[cc][:, dcol:dcol + 128], pbk[b][:, 0:128]))
                G(lambda e, cc=c: e.tensor_copy(dstb[cc][:, dcol:dcol + 128], dstf[cc][:, dcol:dcol + 128]))
            else:
                V(lambda e, cc=c, b=bk: e.tensor_copy(dstb[cc][:, dcol:dcol + 128], pbk[b][:, 0:128]))

    def ffn_attn(l, src, dst, first_layer):
        b0 = 0 if first_layer else 512
        for c in range(8):
            LDMA(stgF[:, 0:1536], rd(wbig[l], 128 * c, 128 * c + 128, 0, 1536))
            G(lambda e, cc=c: e.tensor_copy(w_qkv[cc][:, :], stgF[:, 0:1536]))
        for c in range(8):
            LDMA(stgF[:, 0:1024], rd(wrot_in[l], 128 * c, 128 * c + 128, 0, 1024))
            G(lambda e, cc=c: e.tensor_copy(w_rot[cc][:, :], stgF[:, 0:1024]))
        for c in range(4):
            LDMA(stgF[:, 0:1024], rd(wo_in[l], 128 * c, 128 * c + 128, 0, D))
            G(lambda e, cc=c: e.tensor_copy(w_wo[cc][:, :], stgF[:, 0:1024]))
        LDMA(colA[0:64, 0:1], qsc[l].ap()[:, :])
        LDMA(colA[64:128, 0:1], qsc[l].ap()[:, :])
        LDMA(colA[0:32, 1:2], qsc[l].ap()[32:64, :])
        LDMA(colA[32:64, 1:2], qsc[l].ap()[0:32, :])
        LDMA(colA[64:96, 1:2], qsc[l].ap()[32:64, :])
        LDMA(colA[96:128, 1:2], qsc[l].ap()[0:32, :])
        LDMA(colA[0:64, 2:3], ksc[l].ap()[:, :])
        LDMA(colA[64:128, 2:3], ksc[l].ap()[:, :])
        LDMA(colA[0:32, 3:4], ksc[l].ap()[32:64, :])
        LDMA(colA[32:64, 3:4], ksc[l].ap()[0:32, :])
        LDMA(colA[64:96, 3:4], ksc[l].ap()[32:64, :])
        LDMA(colA[96:128, 3:4], ksc[l].ap()[0:32, :])
        G(lambda e: e.memset(vring[:, :], 1.0))

        ntiles = 7 if first_layer else 6
        for i in range(ntiles):
            tau = 512 * i
            s2 = i % 2
            for c in range(8):
                if i == 0:
                    G(lambda e, cc=c: e.memset(xnb[cc][:, 0:1], 0.0))
                else:
                    G(lambda e, cc=c: e.tensor_copy(xnb[cc][:, 0:1], xnb[cc][:, 512:513]))
            if first_layer:
                for s4 in range(4):
                    ln_group(rd(xk, 512 + tau + 128 * s4, 512 + tau + 128 * s4 + 128, 0, D),
                             xnf, xnb, 1 + 128 * s4, True)
            else:
                for c in range(8):
                    LDMA(xnf[c][:, 1:513], src.ap()[c, :, tau:tau + 512])
                    G(lambda e, cc=c: e.tensor_copy(xnb[cc][:, 1:513], xnf[cc][:, 1:513]))
            for ti in range(4):
                LDMA(tabt[ti][:, :], rd(tabs_in, 0, 128, ti * HBUF + b0 + tau, ti * HBUF + b0 + tau + 512))
                V(lambda e, t2=ti: e.tensor_scalar(tabt[t2][:, :], tabt[t2][:, :], colA[:, t2:t2 + 1], None, OP.mult))

            def rhs_x(c):
                return xnb[c][:, (1 if c < 4 else 0):(513 if c < 4 else 512)]

            def qk(which):
                cb = 0 if which == "q" else 512
                t1_, t2_ = (tabt[0], tabt[1]) if which == "q" else (tabt[2], tabt[3])
                for pp in range(4):
                    for c in range(8):
                        T(lambda e, cc=c, q=pp: e.matmul(pbk[q % 2][:, :], w_qkv[cc][:, cb + 128 * q:cb + 128 * q + 128], rhs_x(cc), start=(cc == 0), stop=(cc == 7)))
                    SC(lambda e, q=pp: e.copy(qA[q][:, :], pbk[q % 2][:, :]))
                    V(lambda e, q=pp: e.tensor_tensor(sqb[:, :], qA[q][:, :], qA[q][:, :], OP.mult))
                    T(lambda e, q=pp: e.matmul(pbk[2][0:8, :], c_oneh[:, 8 * q:8 * q + 8], sqb[:, :], start=(q == 0), stop=(q == 3)))
                SC(lambda e: e.activation(r8a[:, :], pbk[2][0:8, :], AF.Ln, bias=1e-24, scale=1.0))
                SC(lambda e: e.activation(r8a[:, :], r8a[:, :], AF.Exp, bias=0.0, scale=-0.5))
                for pp in range(4):
                    for c in range(8):
                        T(lambda e, cc=c, q=pp: e.matmul(pbk[4 + q][:, :], w_rot[cc][:, cb + 128 * q:cb + 128 * q + 128], rhs_x(cc), start=(cc == 0), stop=(cc == 7)))
                for pp in range(4):
                    T(lambda e, q=pp: e.matmul(pbk[3][:, :], c_bc8[:, 128 * q:128 * q + 128], r8a[:, :], start=True, stop=True))
                    V(lambda e, q=pp, ta=t1_: e.tensor_tensor(sqb[:, :], qA[q][:, :], ta[:, :], OP.mult))
                    V(lambda e, q=pp, tb=t2_: e.tensor_tensor(tmpb[:, :], pbk[4 + q][:, :], tb[:, :], OP.mult))
                    V(lambda e: e.tensor_add(sqb[:, :], sqb[:, :], tmpb[:, :]))
                    if which == "q":
                        V(lambda e, q=pp: e.tensor_tensor(qbuf[q][:, :], sqb[:, :], pbk[3][:, :], OP.mult))
                    else:
                        V(lambda e, q=pp, sl=s2: e.tensor_tensor(kring[q][:, 512 * sl:512 * sl + 512], sqb[:, :], pbk[3][:, :], OP.mult))

            qk("k")
            do_attn = (not first_layer) or (i >= 1)
            if do_attn:
                qk("q")
            for sub in range(4):
                for c in range(8):
                    off = (1 if c < 4 else 0) + 128 * sub
                    for pp in range(4):
                        T(lambda e, cc=c, s=sub, q=pp, o=off: e.matmul(pbk[4 + q][:, 128 * s:128 * s + 128], xnb[cc][:, o:o + 128], w_qkv[cc][:, 1024 + 128 * q:1024 + 128 * q + 128], start=(cc == 0), stop=(cc == 7)))
            for pp in range(4):
                dst520 = vring[:, 520 * s2:520 * s2 + 520] if False else None
                V(lambda e, q=pp, s=s2: e.tensor_copy(
                    kring[q][:, :][:, 0:0] if False else
                    vring[:, 520 * (2 * q + s):520 * (2 * q + s) + 520].rearrange("p (s4 h d) -> p s4 h d", s4=4, h=2, d=65)[:, :, :, 0:64],
                    pbk[4 + q][:, :].rearrange("p (s4 h d) -> p s4 h d", s4=4, h=2, d=64)))
            if (not first_layer):
                LDMA(c_vm[:, :], vmask.ap()[:, tau:tau + 512])
            if do_attn:
                clo = max(0, 4 * i - 4)
                chunks = list(range(clo, 4 * i + 4))
                for hh in range(8):
                    pp = hh // 2
                    half = hh % 2
                    h0 = 64 * half
                    for j0 in range(0, len(chunks), 4):
                        grp = chunks[j0:j0 + 4]
                        for j, ck in enumerate(grp):
                            T(lambda e, q=pp, hz=h0, sl=(ck // 4) % 2, k2=ck % 4, b=j:
                              e.matmul(pbk[b][:, :], kring[q][hz:hz + 64, 512 * sl + 128 * k2:512 * sl + 128 * k2 + 128], qbuf[q][hz:hz + 64, :], start=True, stop=True))
                        for j, ck in enumerate(grp):
                            SC(lambda e, b=j, bc=ck + b0 // 128: e.activation(ptile[b][:, :], pbk[b][:, :], AF.Exp, bias=c_fb[:, bc:bc + 1], scale=8.0))
                        for j, ck in enumerate(grp):
                            if ck >= 4 * i:
                                G(lambda e, b=j, bs=tau - 128 * ck: e.affine_select(out=ptile[b][:, :], in_=ptile[b][:, :], pattern=[[1, 512]], compare_op=OP.is_ge, fill=0.0, base=bs, channel_multiplier=-1))
                            else:
                                G(lambda e, b=j, bs=128 * ck - tau + 512: e.affine_select(out=ptile[b][:, :], in_=ptile[b][:, :], pattern=[[-1, 512]], compare_op=OP.is_ge, fill=0.0, base=bs, channel_multiplier=1))
                        for j, ck in enumerate(grp):
                            T(lambda e, q=pp, sl=(ck // 4) % 2, k2=ck % 4, h=half, b=j, st=(j0 == 0 and j == 0), sp=(j0 + j == len(chunks) - 1):
                              e.matmul(pbk[7][0:65, :], vring[:, 520 * (2 * q + sl) + 130 * k2 + 65 * h:520 * (2 * q + sl) + 130 * k2 + 65 * h + 65], ptile[b][:, :], start=st, stop=sp))
                    V(lambda e: e.tensor_copy(r_a[:, :], pbk[7][64:65, :]))
                    SC(lambda e: e.activation(r_b[:, :], r_a[:, :], AF.Ln, bias=1e-30, scale=1.0))
                    SC(lambda e: e.activation(r_b[:, :], r_b[:, :], AF.Exp, bias=0.0, scale=-1.0))
                    T(lambda e: e.matmul(pbk[6][0:64, :], c_1x[0:1, 0:64], r_b[:, :], start=True, stop=True))
                    V(lambda e: e.tensor_copy(rbsb[:, :], pbk[6][0:64, :]))
                    V(lambda e, c2=pp, h=half: e.tensor_tensor(ofb[c2][64 * h:64 * h + 64, :], pbk[7][0:64, :], rbsb[:, :], OP.mult))
                for c in range(8):
                    for ic in range(4):
                        T(lambda e, c2=ic, cc=c, b=c: e.matmul(pbk[b][:, :], w_wo[c2][:, 128 * cc:128 * cc + 128], ofb[c2][:, :], start=(c2 == 0), stop=(c2 == 3)))
                for c in range(8):
                    rs = c % 2
                    V(lambda e, cc=c, b=c, r2=rs: e.tensor_add(resb[r2][:, :], pbk[b][:, :], xnf[cc][:, 1:513]))
                    if not first_layer:
                        V(lambda e, r2=rs: e.tensor_tensor(resb[r2][:, :], resb[r2][:, :], c_vm[:, :], OP.mult))
                    SDMA(dst.ap()[c, :, tau - (512 if first_layer else 0):tau - (512 if first_layer else 0) + 512], resb[rs][:, :])

    def mlp(l, srcscr, hnscr, prtscr, dstscr, jt, scol0, mask_dst):
        LDMA(lg_s[:, :], lgrow[l].ap()[:, :])
        LDMA(lb_s[:, :], lbrow[l].ap()[:, :])
        LDMA(b2_s[:, :], b2cin[l].ap()[:, :])
        LDMA(b1_s[:, :], b1c[l].ap()[:, :])
        for j in range(jt):
            sc = scol0 + 512 * j
            for c in range(8):
                LDMA(xnf[c][:, 1:513], srcscr.ap()[c, :, sc:sc + 512])
                if sc == 0:
                    G(lambda e, cc=c: e.memset(xnf[cc][:, 0:1], 0.0))
                else:
                    LDMA(xnf[c][:, 0:1], srcscr.ap()[c, :, sc - 1:sc], slow=True)

            def hsl(c):
                return xnf[c][:, (1 if c < 4 else 0):(513 if c < 4 else 512)]

            for c in range(8):
                V(lambda e, cc=c: e.tensor_tensor(sqb[:, :], hsl(cc), hsl(cc), OP.mult))
                T(lambda e, cc=c: e.matmul(pbk[2][0:1, :], c_1col[:, :], hsl(cc), start=(cc == 0), stop=(cc == 7)))
                T(lambda e, cc=c: e.matmul(pbk[5][0:1, :], c_1col[:, :], sqb[:, :], start=(cc == 0), stop=(cc == 7)))
            V(lambda e: e.tensor_scalar(r_m[:, :], pbk[2][0:1, :], 1.0 / D, None, OP.mult))
            V(lambda e: e.tensor_tensor(r_a[:, :], r_m[:, :], r_m[:, :], OP.mult))
            V(lambda e: e.scalar_tensor_tensor(r_a[:, :], pbk[5][0:1, :], 1.0 / D, r_a[:, :], OP.mult, OP.subtract))
            SC(lambda e: e.activation(r_b[:, :], r_a[:, :], AF.Ln, bias=1e-5, scale=1.0))
            SC(lambda e: e.activation(r_b[:, :], r_b[:, :], AF.Exp, bias=0.0, scale=-0.5))
            V(lambda e: e.scalar_tensor_tensor(r_a[:, :], r_m[:, :], -1.0, r_b[:, :], OP.mult, OP.mult))
            for c in range(8):
                T(lambda e, cc=c: e.matmul(pbk[3][:, :], lg_s[0:1, 128 * cc:128 * cc + 128], r_b[:, :], start=True, stop=True))
                T(lambda e, cc=c: e.matmul(pbk[4][:, :], lg_s[0:1, 128 * cc:128 * cc + 128], r_a[:, :], start=True, stop=False))
                T(lambda e, cc=c: e.matmul(pbk[4][:, :], lb_s[0:1, 128 * cc:128 * cc + 128], c_orow[:, :], start=False, stop=True))
                V(lambda e, cc=c: e.tensor_tensor(tmpb[:, :], hsl(cc), pbk[3][:, :], OP.mult))
                V(lambda e: e.tensor_add(sqb[:, :], tmpb[:, :], pbk[4][:, :]))
                G(lambda e, cc=c: e.tensor_copy(hnb[cc][:, :], sqb[:, :]))
                SDMA(hnscr.ap()[c, :, 512 * j:512 * j + 512], hnb[c][:, :])
        for half in range(2):
            for c in range(8):
                LDMA(stgF[:, 0:2816], rd(w1p[l], 128 * c, 128 * c + 128, 2816 * half, 2816 * half + 2816))
                G(lambda e, cc=c: e.tensor_copy(w1h[cc][:, :], stgF[:, 0:2816]))
            for mc in range(11):
                LDMA(stgF[:, 0:1024], rd(w2p[l], 128 * (11 * half + mc), 128 * (11 * half + mc) + 128, 0, D))
                G(lambda e, m=mc: e.tensor_copy(w2h[:, 1024 * m:1024 * m + 1024], stgF[:, 0:1024]))
            for j in range(jt):
                sc = scol0 + 512 * j
                if half == 1 and mask_dst:
                    LDMA(c_vm[:, :], vmask.ap()[:, 512 * j:512 * j + 512])
                for c in range(8):
                    LDMA(hnb[c][:, :], hnscr.ap()[c, :, 512 * j:512 * j + 512])
                for mc0 in range(0, 11, 2):
                    grp = [m for m in (mc0, mc0 + 1) if m < 11]
                    for gi, mc in enumerate(grp):
                        for c in range(8):
                            T(lambda e, cc=c, m=mc, b=2 * gi: e.matmul(pbk[b][:, :], w1h[cc][:, 256 * m:256 * m + 128], hnb[cc][:, :], start=(cc == 0), stop=(cc == 7)))
                        for c in range(8):
                            T(lambda e, cc=c, m=mc, b=2 * gi + 1: e.matmul(pbk[b][:, :], w1h[cc][:, 256 * m + 128:256 * m + 256], hnb[cc][:, :], start=(cc == 0), stop=(cc == 7)))
                    for gi, mc in enumerate(grp):
                        ma = 2 * (11 * half + mc)
                        SC(lambda e, m2=ma, b=2 * gi + 1, g2=gi: e.activation(gelb[:, 0:512] if g2 == 0 else gel2[:, 0:512], pbk[b][:, :], AF.Gelu, bias=b1_s[:, m2 + 1:m2 + 2], scale=1.0))
                    for gi, mc in enumerate(grp):
                        ma = 2 * (11 * half + mc)
                        V(lambda e, m=mc, m2=ma, b=2 * gi, g2=gi: e.scalar_tensor_tensor(prodb[m][:, :], pbk[b][:, :], b1_s[:, m2:m2 + 1], gelb[:, 0:512] if g2 == 0 else gel2[:, 0:512], OP.add, OP.mult))
                for c in range(8):
                    bk = 4 + (c % 4)
                    for mc2 in range(11):
                        T(lambda e, m=mc2, cc=c, b=bk: e.matmul(pbk[b][:, :], w2h[:, 1024 * m + 128 * cc:1024 * m + 128 * cc + 128], prodb[m][:, :], start=(m == 0), stop=(m == 10)))
                    ss = c % 2
                    if half == 0:
                        V(lambda e, b=bk, s=ss: e.tensor_copy(resb[s][:, :], pbk[b][:, :]))
                        SDMA(prtscr.ap()[c, :, 512 * j:512 * j + 512], resb[ss][:, :])
                    else:
                        LDMA(tmpb[:, :], prtscr.ap()[c, :, 512 * j:512 * j + 512])
                        LDMA(sqb[:, :], srcscr.ap()[c, :, sc:sc + 512])
                        V(lambda e, b=bk, s=ss: e.tensor_add(resb[s][:, :], pbk[b][:, :], tmpb[:, :]))
                        V(lambda e, s=ss: e.tensor_add(resb[s][:, :], resb[s][:, :], sqb[:, :]))
                        V(lambda e, s=ss, cc=c: e.tensor_scalar(resb[s][:, :], resb[s][:, :], b2_s[:, cc:cc + 1], None, OP.add))
                        if mask_dst:
                            V(lambda e, s=ss: e.tensor_tensor(resb[s][:, :], resb[s][:, :], c_vm[:, :], OP.mult))
                        SDMA(dstscr.ap()[c, :, 512 * j:512 * j + 512], resb[ss][:, :])

    def attn_phase():
        G(lambda e: e.memset(vall[:, :], 1.0))
        for ph in range(2):
            for c in range(8):
                LDMA(stgF[:, 0:512], rd(wv_att, 128 * c, 128 * c + 128, 512 * ph, 512 * ph + 512))
                G(lambda e, cc=c: e.tensor_copy(wv_h[cc][:, :], stgF[:, 0:512]))
            for tg in range(8):
                if ph == 0:
                    for s4 in range(4):
                        ln_group(rd(xk, 512 * tg + 128 * s4, 512 * tg + 128 * s4 + 128, 0, D), None, xnt8, 128 * s4, False)
                    for c in range(8):
                        SDMA(xns.ap()[c, :, 512 * tg:512 * tg + 512], xnt8[c][:, :])
                else:
                    for c in range(8):
                        LDMA(xnt8[c][:, :], xns.ap()[c, :, 512 * tg:512 * tg + 512])
                for sub in range(4):
                    for c in range(8):
                        T(lambda e, cc=c, s=sub: e.matmul(pbk[4][:, :], xnt8[cc][:, 128 * s:128 * s + 128], wv_h[cc][:, :], start=(cc == 0), stop=(cc == 7)))
                    tc = 4 * tg + sub
                    V(lambda e, t2=tc, p2=ph: e.tensor_copy(
                        vall[:, 1040 * t2 + 520 * p2:1040 * t2 + 520 * p2 + 520].rearrange("p (q h d) -> p q h d", q=4, h=2, d=65)[:, :, :, 0:64],
                        pbk[4][:, :].rearrange("p (q h d) -> p q h d", q=4, h=2, d=64)))
        for pp in range(8):
            for c in range(8):
                LDMA(stgF[:, 0:128], rd(wq_att, 128 * c, 128 * c + 128, 128 * pp, 128 * pp + 128))
                LDMA(stgF[:, 128:256], rd(wk_att, 128 * c, 128 * c + 128, 128 * pp, 128 * pp + 128))
                G(lambda e, cc=c: e.tensor_copy(wqk_a[cc][:, :], stgF[:, 0:256]))
            for kt in range(8):
                for c in range(8):
                    LDMA(xnt8[c][:, :], xns.ap()[c, :, 512 * kt:512 * kt + 512])
                for c in range(8):
                    T(lambda e, cc=c: e.matmul(pbk[0][:, :], wqk_a[cc][:, 128:256], xnt8[cc][:, :], start=(cc == 0), stop=(cc == 7)))
                SC(lambda e, k2=kt: e.copy(kpr[:, 512 * k2:512 * k2 + 512], pbk[0][:, :]))
                if kt >= 4:
                    for c in range(8):
                        T(lambda e, cc=c: e.matmul(pbk[1][:, :], wqk_a[cc][:, 0:128], xnt8[cc][:, :], start=(cc == 0), stop=(cc == 7)))
                    SC(lambda e, k2=kt - 4: e.copy(qpr[:, 512 * k2:512 * k2 + 512], pbk[1][:, :]))
            for hh in range(2):
                h0 = 64 * hh
                for qt in range(4):
                    kmax = 20 + 4 * qt
                    for j0 in range(0, kmax, 6):
                        grp = list(range(j0, min(j0 + 6, kmax)))
                        for j, ck in enumerate(grp):
                            T(lambda e, k2=ck, b=j, h=h0, q=qt: e.matmul(pbk[b][:, :], kpr[h:h + 64, 128 * k2:128 * k2 + 128], qpr[h:h + 64, 512 * q:512 * q + 512], start=True, stop=True))
                        for j, ck in enumerate(grp):
                            SC(lambda e, b=j, k2=ck: e.activation(aptile[b][:, :], pbk[b][:, :], AF.Exp, bias=c_kb[:, k2:k2 + 1], scale=0.125))
                        for j, ck in enumerate(grp):
                            if ck >= kmax - 4:
                                G(lambda e, b=j, bs=2048 + 512 * qt - 128 * ck: e.affine_select(out=aptile[b][:, :], in_=aptile[b][:, :], pattern=[[1, 512]], compare_op=OP.is_ge, fill=0.0, base=bs, channel_multiplier=-1))
                        for j, ck in enumerate(grp):
                            T(lambda e, k2=ck, h=hh, b=j, q=pp, st=(j0 == 0 and j == 0), sp=(j0 + j == kmax - 1):
                              e.matmul(pbk[7][0:65, :], vall[:, 1040 * k2 + 520 * (q // 4) + 130 * (q % 4) + 65 * h:1040 * k2 + 520 * (q // 4) + 130 * (q % 4) + 65 * h + 65], aptile[b][:, :], start=st, stop=sp))
                    V(lambda e: e.tensor_copy(r_a[:, :], pbk[7][64:65, :]))
                    SC(lambda e: e.activation(r_b[:, :], r_a[:, :], AF.Ln, bias=1e-30, scale=1.0))
                    SC(lambda e: e.activation(r_b[:, :], r_b[:, :], AF.Exp, bias=0.0, scale=-1.0))
                    T(lambda e: e.matmul(pbk[6][0:64, :], c_1x[0:1, 0:64], r_b[:, :], start=True, stop=True))
                    V(lambda e: e.tensor_copy(rbsb[:, :], pbk[6][0:64, :]))
                    V(lambda e, h=h0: e.tensor_tensor(osm[h:h + 64, :], pbk[7][0:64, :], rbsb[:, :], OP.mult))
                    SDMA(oalls.ap()[pp, 64 * hh:64 * hh + 64, 512 * qt:512 * qt + 512], osm[64 * hh:64 * hh + 64, :])
        for qt in range(4):
            for ic in range(8):
                LDMA(oqt[ic][:, :], oalls.ap()[ic, :, 512 * qt:512 * qt + 512])
            for wh in range(2):
                for c2 in range(4):
                    cc = 4 * wh + c2
                    LDMA(stgF[:, 0:1024], rd(wcat_in, 128 * cc, 128 * cc + 128, 0, D))
                    G(lambda e, c3=c2: e.tensor_copy(wcat_s[c3][:, :], stgF[:, 0:1024]))
                for c in range(8):
                    for ic in range(4):
                        T(lambda e, c2=ic, cc=c, b=c, w=wh: e.matmul(pbk[b][:, :], wcat_s[c2][:, 128 * cc:128 * cc + 128], oqt[4 * w + c2][:, :], start=(c2 == 0 and w == 0), stop=(c2 == 3 and w == 1)))
                if wh == 0:
                    continue
                for c in range(8):
                    LDMA(tmpb[:, :], Fs.ap()[c, :, 512 * qt:512 * qt + 512])
                    V(lambda e, b=c, cc=c: e.scalar_tensor_tensor(xnf[cc][:, 0:512], pbk[b][:, :], bo_s[:, cc:cc + 1], tmpb[:, :], OP.add, OP.add))
            for s4 in range(4):
                for dmg in range(2):
                    for c2 in range(4):
                        cc = 4 * dmg + c2
                        T(lambda e, c3=cc, s=s4, b=2 + dmg, c4=c2: e.transpose(pbk[b][:, 128 * c4:128 * c4 + 128], xnf[c3][:, 128 * s:128 * s + 128], c_eyef[:, :]))
                LDMA(xtm[:, :], rd(xk, 2048 + 512 * qt + 128 * s4, 2048 + 512 * qt + 128 * s4 + 128, 0, D))
                V(lambda e: e.tensor_add(resb[0][:, :], pbk[2][:, :], xtm[:, 0:512]))
                V(lambda e: e.tensor_add(resb[1][:, :], pbk[3][:, :], xtm[:, 512:1024]))
                SDMA(out.ap()[512 * qt + 128 * s4:512 * qt + 128 * s4 + 128, 0:512], resb[0][:, :])
                SDMA(out.ap()[512 * qt + 128 * s4:512 * qt + 128 * s4 + 128, 512:1024], resb[1][:, :])

    wv_h = [wv_a[i // 2][:, 512 * (i % 2):512 * (i % 2) + 512] for i in range(8)]

    ffn_attn(0, None, a1s, True)
    mlp(0, a1s, hn1, prt, x2s, 6, 0, True)
    ffn_attn(1, x2s, a2s, False)
    mlp(1, a2s, hn2, prt2, Fs, 4, 1024, False)
    attn_phase()
    p.step("sync", None)

    import contextlib
    stack = contextlib.ExitStack()
    sems = {}
    for s in ("T", "V", "S", "G", "Y", "LD", "ST"):
        sems[s] = stack.enter_context(nc.semaphore(f"sem{s}"))
    blk = stack.enter_context(nc.Block())

    def mk(engname):
        def body(eng):
            for fn, waits, sem, amt in p.streams[engname]:
                for (s, v) in waits:
                    eng.wait_ge(sems[s], v)
                if fn is not None:
                    fn(eng).then_inc(sems[sem], amt)
        return body

    blk.tensor(mk("tensor"))
    blk.vector(mk("vector"))
    blk.scalar(mk("scalar"))
    blk.gpsimd(mk("gpsimd"))
    blk.sync(mk("sync"))
    stack.close()
    return nc


def _tables(core_h):
    toff = 512 if core_h == 1 else -1536
    t = np.arange(HBUF, dtype=np.float64) + toff
    d = np.arange(64)
    half = d % 32
    inv_freq = 1.0 / (10000.0 ** (2 * half / 64.0))
    freqs = t[:, None] * inv_freq[None, :]
    base = (2 * half + 0.4 * 64) / (1.4 * 64)
    power = (t - 2048.0) / 256.0
    xsc = base[None, :] ** power[:, None]
    sgn = np.where(d < 32, -1.0, 1.0)
    t1q = np.cos(freqs) * xsc
    t2q = np.sin(freqs) * xsc * sgn[None, :]
    t1k = np.cos(freqs) / xsc
    t2k = np.sin(freqs) / xsc * sgn[None, :]
    tab = np.concatenate([t1q.T, t2q.T, t1k.T, t2k.T], axis=1).astype(np.float32)
    return np.ascontiguousarray(np.concatenate([tab, tab], axis=0))


_NC_CACHE = {}
_LAST_RES = None


def kernel(L0_attn_resid, L0_attn_W_Q, L0_attn_W_K, L0_attn_W_V, L0_attn_W_out,
           L0_attn_b_out, L0_ffn_wqkv, L0_ffn_q_scale, L0_ffn_k_scale, L0_ffn_wo,
           L0_ffn_ln_g, L0_ffn_ln_b, L0_ffn_w1, L0_ffn_b1, L0_ffn_w2, L0_ffn_b2):
    x = np.ascontiguousarray(np.asarray(L0_attn_resid, np.float32))
    f32 = lambda a: np.ascontiguousarray(np.asarray(a, np.float32))
    sh = {}
    sh["eyef"] = np.eye(128, dtype=np.float32)
    oneh = np.zeros((128, 32), np.float32)
    for pp in range(4):
        oneh[0:64, 8 * pp + 2 * pp] = 1.0
        oneh[64:128, 8 * pp + 2 * pp + 1] = 1.0
    sh["onehots"] = oneh
    sh["onescol"] = np.ones((128, 1), np.float32)
    bc8 = np.zeros((8, 512), np.float32)
    for pp in range(4):
        bc8[2 * pp, 128 * pp:128 * pp + 64] = 1.0
        bc8[2 * pp + 1, 128 * pp + 64:128 * pp + 128] = 1.0
    sh["bcast8"] = bc8
    sh["ones1x"] = np.ones((1, 128), np.float32)
    sh["onesrow"] = np.ones((1, 512), np.float32)
    sh["negio"] = np.tile(-np.arange(512, dtype=np.float32)[None, :], (128, 1))
    sh["constv"] = np.tile(np.array([1e-5, 1e-24, 1e-30, 0.0], np.float32)[None, :], (128, 1))
    wqkv = f32(L0_ffn_wqkv)
    perm = (np.arange(64) + 32) % 64
    for l in range(2):
        sh[f"wbig{l}"] = np.ascontiguousarray(wqkv[l])
        rot = np.zeros((D, 1024), np.float32)
        rot[:, 0:512] = wqkv[l][:, 0:512].reshape(D, 8, 64)[:, :, perm].reshape(D, 512)
        rot[:, 512:1024] = wqkv[l][:, 512:1024].reshape(D, 8, 64)[:, :, perm].reshape(D, 512)
        sh[f"wrot{l}"] = rot
        sh[f"wo{l}"] = f32(L0_ffn_wo)[l]
        w1 = f32(L0_ffn_w1)[l]
        b1 = f32(L0_ffn_b1)[l]
        a = np.zeros((D, MP * 128), np.float32)
        g = np.zeros((D, MP * 128), np.float32)
        a[:, 0:2730] = w1[:, 0:2730]
        g[:, 0:2730] = w1[:, 2730:5460]
        ba = np.zeros(MP * 128, np.float32)
        bg = np.zeros(MP * 128, np.float32)
        ba[0:2730] = b1[0:2730]
        bg[0:2730] = b1[2730:5460]
        w1pp = np.zeros((D, MP * 256), np.float32)
        b1cc = np.zeros((128, 2 * MP), np.float32)
        for m in range(MP):
            w1pp[:, 256 * m:256 * m + 128] = a[:, 128 * m:128 * m + 128]
            w1pp[:, 256 * m + 128:256 * m + 256] = g[:, 128 * m:128 * m + 128]
            b1cc[:, 2 * m] = ba[128 * m:128 * m + 128]
            b1cc[:, 2 * m + 1] = bg[128 * m:128 * m + 128]
        sh[f"w1p{l}"] = w1pp
        sh[f"b1c{l}"] = b1cc
        w2pp = np.zeros((MP * 128, D), np.float32)
        w2pp[0:2730, :] = f32(L0_ffn_w2)[l]
        sh[f"w2p{l}"] = w2pp
        sh[f"b2cin{l}"] = np.ascontiguousarray(f32(L0_ffn_b2)[l].reshape(8, 128).T)
        sh[f"lgrow{l}"] = f32(L0_ffn_ln_g)[l][None, :]
        sh[f"lbrow{l}"] = f32(L0_ffn_ln_b)[l][None, :]
        sh[f"qsc{l}"] = f32(L0_ffn_q_scale)[l][:, None]
        sh[f"ksc{l}"] = f32(L0_ffn_k_scale)[l][:, None]
    sh["wq_att"] = f32(L0_attn_W_Q).reshape(D, D)
    sh["wk_att"] = f32(L0_attn_W_K).reshape(D, D)
    sh["wv_att"] = f32(L0_attn_W_V).reshape(D, D)
    sh["wcat"] = np.ascontiguousarray(f32(L0_attn_W_out).transpose(1, 0, 2).reshape(D, D))
    sh["bocin"] = np.ascontiguousarray(f32(L0_attn_b_out).reshape(8, 128).T)

    tab_lo, tab_hi = _tables(0), _tables(1)
    kb_lo = np.full((128, 32), -24.0, np.float32)
    kb_lo[:, 0:16] = -1e9
    kb_hi = np.full((128, 32), -24.0, np.float32)
    fb_lo = np.full((128, 28), -32.0, np.float32)
    fb_lo[:, 0:12] = -1e9
    fb_hi = np.full((128, 28), -32.0, np.float32)
    vl_lo = np.full((128, 1), -1535.5, np.float32)
    vl_hi = np.full((128, 1), 1e9, np.float32)

    if "nc" not in _NC_CACHE:
        _NC_CACHE["nc"] = build_nc()
    nc = _NC_CACHE["nc"]
    in_maps = []
    for core in range(8):
        b, h = core // 2, core % 2
        xkc = x[b] if h == 1 else np.concatenate([np.zeros((2048, D), np.float32), x[b, 0:2048]], axis=0)
        vm = np.ones((128, 3072), np.float32)
        if h == 0:
            vm[:, 0:1024] = 0.0
        m = {"vmask": vm,
             "xk": np.ascontiguousarray(xkc),
             "kbias": kb_hi if h == 1 else kb_lo,
             "fbias": fb_hi if h == 1 else fb_lo,
             "tabs": tab_hi if h == 1 else tab_lo,
             "vlen": vl_hi if h == 1 else vl_lo}
        m.update(sh)
        in_maps.append(m)
    res = run_bass_kernel_spmd(nc, in_maps, core_ids=list(range(8)))
    global _LAST_RES
    _LAST_RES = res
    o = np.zeros((B, S, D), np.float32)
    for core in range(8):
        b, h = core // 2, core % 2
        o[b, 2048 * h:2048 * h + 2048] = res.results[core]["out"]
    return o


# version 08 (final)
# speedup vs baseline: 1.1594x; 1.0083x over previous
import numpy as np
import concourse.bass as bass
import concourse.mybir as mybir
from concourse.bass_utils import run_bass_kernel_spmd

F32 = mybir.dt.float32
BF16 = mybir.dt.bfloat16
AF = mybir.ActivationFunctionType
OP = mybir.AluOpType
AX = mybir.AxisListType

B, S, D = 4, 4096, 1024
HBUF = 3584
MP = 22
ENGINES = ("tensor", "vector", "scalar", "gpsimd", "sync")
ESEM = {"tensor": "T", "vector": "V", "scalar": "S", "gpsimd": "G", "sync": "Y"}


class Prog:
    def __init__(self):
        self.streams = {e: [] for e in ENGINES}
        self.cnt = {k: 0 for k in ("T", "V", "S", "G", "Y", "LD", "ST")}
        self.hw = {}
        self.last = None

    def step(self, eng, fn, sem=None, amt=1):
        sem = sem or ESEM[eng]
        waits = []
        t = self.last
        if t is not None:
            s, v = t
            if not (s == ESEM[eng]) and not (sem in ("LD", "ST") and s == sem):
                k = (eng, s)
                if self.hw.get(k, 0) < v:
                    self.hw[k] = v
                    waits.append((s, v))
        self.cnt[sem] += amt
        tok = (sem, self.cnt[sem])
        self.streams[eng].append((fn, waits, sem, amt))
        self.last = tok
        return tok
    def fence(self, eng):
        sem = ESEM[eng]
        v = self.cnt[sem]
        if v > 0:
            self.streams[eng].append((None, [(sem, v)], sem, 0))



def build_nc(debug=False):
    nc = bass.Bass("TRN2", target_bir_lowering=False)
    p = Prog()

    def din(name, shape, dt=F32):
        return nc.dram_tensor(name, shape, dt, kind="ExternalInput")

    xk = din("xk", [S, D])
    kbias = din("kbias", [128, 32])
    fbias = din("fbias", [128, 28])
    tabs_in = din("tabs", [128, 4 * HBUF])
    vlen = din("vlen", [128, 1])
    eyef = din("eyef", [128, 128])
    onehots = din("onehots", [128, 32])
    onescol = din("onescol", [128, 1])
    bcast8 = din("bcast8", [8, 512])
    ones1x = din("ones1x", [1, 128])
    onesrow = din("onesrow", [1, 512])
    negio = din("negio", [128, 512])
    constv = din("constv", [128, 4])
    vmask = din("vmask", [128, 3072])
    wbig = [din(f"wbig{l}", [D, 1536]) for l in range(2)]
    wrot_in = [din(f"wrot{l}", [D, 1024]) for l in range(2)]
    wo_in = [din(f"wo{l}", [512, D]) for l in range(2)]
    w1p = [din(f"w1p{l}", [D, MP * 256]) for l in range(2)]
    w2p = [din(f"w2p{l}", [MP * 128, D]) for l in range(2)]
    b1c = [din(f"b1c{l}", [128, 2 * MP]) for l in range(2)]
    b2cin = [din(f"b2cin{l}", [128, 8]) for l in range(2)]
    lgrow = [din(f"lgrow{l}", [1, D]) for l in range(2)]
    lbrow = [din(f"lbrow{l}", [1, D]) for l in range(2)]
    qsc = [din(f"qsc{l}", [64, 1]) for l in range(2)]
    ksc = [din(f"ksc{l}", [64, 1]) for l in range(2)]
    wq_att = din("wq_att", [D, D])
    wk_att = din("wk_att", [D, D])
    wv_att = din("wv_att", [D, D])
    wcat_in = din("wcat", [D, D])
    bocin = din("bocin", [128, 8])
    out = nc.dram_tensor("out", [2048, D], F32, kind="ExternalOutput")

    kd = dict(kind="ExternalOutput") if debug else {}
    a1s = nc.dram_tensor("a1s", [8, 128, 3072], F32, **kd)
    hn1 = nc.dram_tensor("hn1", [8, 128, 3072], BF16, **kd)
    x2s = nc.dram_tensor("x2s", [8, 128, 3072], F32, **kd)
    prt = nc.dram_tensor("prt", [8, 128, 3072], F32)
    a2s = nc.dram_tensor("a2s", [8, 128, 3072], F32, **kd)
    hn2 = nc.dram_tensor("hn2", [8, 128, 2048], BF16)
    prt2 = nc.dram_tensor("prt2", [8, 128, 2048], F32)
    Fs = nc.dram_tensor("Fs", [8, 128, 2048], F32, **kd)
    xns = nc.dram_tensor("xns", [8, 128, 4096], BF16, **kd)
    oalls = nc.dram_tensor("oalls", [8, 128, 2048], BF16, **kd)

    AB = nc.alloc_sbuf_tensor("AB", [128, 57472], BF16)
    AFa = nc.alloc_sbuf_tensor("AFa", [128, 11784], F32)
    c_eyef = nc.alloc_sbuf_tensor("c_eyef", [128, 128], F32)
    c_oneh = nc.alloc_sbuf_tensor("c_oneh", [128, 32], F32)
    c_1col = nc.alloc_sbuf_tensor("c_1col", [128, 1], F32)
    c_bc8 = nc.alloc_sbuf_tensor("c_bc8", [8, 512], F32)
    c_1x = nc.alloc_sbuf_tensor("c_1x", [1, 128], F32)
    c_orow = nc.alloc_sbuf_tensor("c_orow", [1, 512], F32)
    c_kb = nc.alloc_sbuf_tensor("c_kb", [128, 32], F32)
    c_fb = nc.alloc_sbuf_tensor("c_fb", [128, 28], F32)
    c_ni = nc.alloc_sbuf_tensor("c_ni", [128, 512], F32)
    c_vl = nc.alloc_sbuf_tensor("c_vl", [128, 1], F32)
    rowst = nc.alloc_sbuf_tensor("rowst", [128, 8], F32)
    r_m = nc.alloc_sbuf_tensor("r_m", [1, 512], F32)
    r_a = nc.alloc_sbuf_tensor("r_a", [1, 512], F32)
    r_b = nc.alloc_sbuf_tensor("r_b", [1, 512], F32)
    r8a = nc.alloc_sbuf_tensor("r8a", [8, 512], F32)
    colA = nc.alloc_sbuf_tensor("colA", [128, 4], F32)
    lg_s = nc.alloc_sbuf_tensor("lg_s", [1, D], F32)
    lb_s = nc.alloc_sbuf_tensor("lb_s", [1, D], F32)
    b2_s = nc.alloc_sbuf_tensor("b2_s", [128, 8], F32)
    b1_s = nc.alloc_sbuf_tensor("b1_s", [128, 2 * MP], F32)
    bo_s = nc.alloc_sbuf_tensor("bo_s", [128, 8], F32)
    c_cv = nc.alloc_sbuf_tensor("c_cv", [128, 4], F32)
    c_vm = nc.alloc_sbuf_tensor("c_vm", [128, 512], F32)
    for _i, _v in enumerate([1e-5, 1e-24, 1e-30]):
        nc.const_aps.register(F32, _v, c_cv[:, _i:_i + 1])

    pbk = [nc.alloc_psum_tensor(f"pbk{i}", [128, 512], F32) for i in range(8)]

    class Arena:
        def __init__(self, t, n):
            self.t, self.n, self.off = t, n, 0

        def take(self, cols):
            o = self.off
            self.off += cols
            assert self.off <= self.n, f"arena overflow {self.off}>{self.n}"
            return self.t[:, o:o + cols]

        def reset(self):
            self.off = 0

    ab = Arena(AB, 57472)
    af = Arena(AFa, 11784)

    xnf = [af.take(513) for _ in range(8)]
    sqb = af.take(512)
    tmpb = af.take(512)
    rbsb_full = af.take(512)
    rbsb = rbsb_full[0:64, :]
    resb = [af.take(512) for _ in range(2)]
    xtm = af.take(1024)
    _shared = af.take(4096)
    qA = [_shared[:, 512 * i:512 * i + 512] for i in range(4)]
    tabt = [_shared[:, 2048 + 512 * i:2048 + 512 * i + 512] for i in range(4)]
    stgF = _shared[:, 0:4096]

    ab.reset()
    w_qkv = [ab.take(1536) for _ in range(8)]
    w_rot = [ab.take(1024) for _ in range(8)]
    w_wo = [ab.take(1024) for _ in range(4)]
    xnb = [ab.take(513) for _ in range(8)]
    kring = [ab.take(1024) for _ in range(4)]
    vring = ab.take(4160)
    qbuf = [ab.take(512) for _ in range(4)]
    ptile = [ab.take(512) for _ in range(4)]
    ofb = [ab.take(512) for _ in range(4)]
    ab.reset()
    w1h = [ab.take(2816) for _ in range(8)]
    w2h = ab.take(11264)
    hnb = [ab.take(512) for _ in range(8)]
    prodb = [ab.take(512) for _ in range(11)]
    gelb = ab.take(512)
    gel2 = ab.take(512)
    ab.reset()
    vall = ab.take(33280)
    xnt8 = [ab.take(512) for _ in range(8)]
    kpr = ab.take(4096)
    qpr = ab.take(2048)
    wv_a = [ab.take(1024) for _ in range(4)]
    wqk_a = [ab.take(256) for _ in range(8)]
    aptile = [ab.take(512) for _ in range(6)]
    oqt = [ab.take(512) for _ in range(8)]
    osm = ab.take(512)
    wcat_s = wv_a

    def T(fn):
        return p.step("tensor", fn)

    def V(fn):
        return p.step("vector", fn)

    def SC(fn):
        return p.step("scalar", fn)

    def G(fn):
        return p.step("gpsimd", fn)

    def LDMA(dst, src, slow=False):
        return p.step("sync", lambda e, d=dst, s=src, sl=slow: e.dma_start(out=d, in_=s, allow_slow_non_contiguous=sl), "LD", 16)

    def SDMA(dst, src):
        return p.step("gpsimd", lambda e, d=dst, s=src: e.dma_start(out=d, in_=s), "ST", 16)

    def rd(t, r0, r1, c0, c1):
        return t.ap()[r0:r1, c0:c1]

    LDMA(c_eyef[:, :], eyef.ap()[:, :])
    LDMA(c_oneh[:, :], onehots.ap()[:, :])
    LDMA(c_1col[:, :], onescol.ap()[:, :])
    LDMA(c_bc8[:, :], bcast8.ap()[:, :])
    LDMA(c_1x[:, :], ones1x.ap()[:, :])
    LDMA(c_orow[:, :], onesrow.ap()[:, :])
    LDMA(c_kb[:, :], kbias.ap()[:, :])
    LDMA(c_fb[:, :], fbias.ap()[:, :])
    LDMA(c_ni[:, :], negio.ap()[:, :])
    LDMA(c_vl[:, :], vlen.ap()[:, :])
    LDMA(bo_s[:, :], bocin.ap()[:, :])
    LDMA(c_cv[:, :], constv.ap()[:, :])

    def ln_group(src_rows_ap, dstf, dstb, dcol, want_f32):
        LDMA(xtm[:, :], src_rows_ap)
        V(lambda e: e.tensor_reduce(rowst[:, 0:1], xtm[:, :], AX.X, OP.add))
        V(lambda e: e.tensor_tensor(sqb[:, :], xtm[:, 0:512], xtm[:, 0:512], OP.mult))
        V(lambda e: e.tensor_reduce(rowst[:, 1:2], sqb[:, :], AX.X, OP.add))
        V(lambda e: e.tensor_tensor(sqb[:, :], xtm[:, 512:1024], xtm[:, 512:1024], OP.mult))
        V(lambda e: e.tensor_reduce(rowst[:, 2:3], sqb[:, :], AX.X, OP.add))
        p.fence("scalar")
        SC(lambda e: e.activation(rowst[:, 3:4], rowst[:, 0:1], AF.Identity, bias=0.0, scale=1.0 / D))
        p.fence("scalar")
        SC(lambda e: e.activation(rowst[:, 5:6], rowst[:, 1:2], AF.Identity, bias=rowst[:, 2:3], scale=1.0))
        p.fence("scalar")
        SC(lambda e: e.activation(rowst[:, 6:7], rowst[:, 3:4], AF.Square, bias=0.0, scale=1.0))
        p.fence("scalar")
        SC(lambda e: e.activation(rowst[:, 6:7], rowst[:, 6:7], AF.Identity, bias=0.0, scale=-1.0))
        p.fence("scalar")
        SC(lambda e: e.activation(rowst[:, 4:5], rowst[:, 5:6], AF.Identity, bias=rowst[:, 6:7], scale=1.0 / D))
        p.fence("scalar")
        SC(lambda e: e.activation(rowst[:, 5:6], rowst[:, 4:5], AF.Ln, bias=1e-5, scale=1.0))
        p.fence("scalar")
        SC(lambda e: e.activation(rowst[:, 4:5], rowst[:, 5:6], AF.Exp, bias=0.0, scale=-0.5))
        p.fence("scalar")
        SC(lambda e: e.activation(rowst[:, 6:7], rowst[:, 3:4], AF.Identity, bias=0.0, scale=rowst[:, 4:5]))
        SC(lambda e: e.activation(rowst[:, 6:7], rowst[:, 6:7], AF.Identity, bias=0.0, scale=-1.0))
        p.fence("scalar")
        SC(lambda e: e.activation(xtm[:, :], xtm[:, :], AF.Identity, bias=rowst[:, 6:7], scale=rowst[:, 4:5]))
        for c in range(8):
            bk = c % 2
            T(lambda e, cc=c, b=bk: e.transpose(pbk[b][:, 0:128], xtm[:, 128 * cc:128 * cc + 128], c_eyef[:, :]))
            if want_f32:
                V(lambda e, cc=c, b=bk: e.tensor_copy(dstf[cc][:, dcol:dcol + 128], pbk[b][:, 0:128]))
                G(lambda e, cc=c: e.tensor_copy(dstb[cc][:, dcol:dcol + 128], dstf[cc][:, dcol:dcol + 128]))
            else:
                V(lambda e, cc=c, b=bk: e.tensor_copy(dstb[cc][:, dcol:dcol + 128], pbk[b][:, 0:128]))

    def ffn_attn(l, src, dst, first_layer):
        b0 = 0 if first_layer else 512
        for c in range(8):
            LDMA(stgF[:, 0:1536], rd(wbig[l], 128 * c, 128 * c + 128, 0, 1536))
            G(lambda e, cc=c: e.tensor_copy(w_qkv[cc][:, :], stgF[:, 0:1536]))
        for c in range(8):
            LDMA(stgF[:, 0:1024], rd(wrot_in[l], 128 * c, 128 * c + 128, 0, 1024))
            G(lambda e, cc=c: e.tensor_copy(w_rot[cc][:, :], stgF[:, 0:1024]))
        for c in range(4):
            LDMA(stgF[:, 0:1024], rd(wo_in[l], 128 * c, 128 * c + 128, 0, D))
            G(lambda e, cc=c: e.tensor_copy(w_wo[cc][:, :], stgF[:, 0:1024]))
        LDMA(colA[0:64, 0:1], qsc[l].ap()[:, :])
        LDMA(colA[64:128, 0:1], qsc[l].ap()[:, :])
        LDMA(colA[0:32, 1:2], qsc[l].ap()[32:64, :])
        LDMA(colA[32:64, 1:2], qsc[l].ap()[0:32, :])
        LDMA(colA[64:96, 1:2], qsc[l].ap()[32:64, :])
        LDMA(colA[96:128, 1:2], qsc[l].ap()[0:32, :])
        LDMA(colA[0:64, 2:3], ksc[l].ap()[:, :])
        LDMA(colA[64:128, 2:3], ksc[l].ap()[:, :])
        LDMA(colA[0:32, 3:4], ksc[l].ap()[32:64, :])
        LDMA(colA[32:64, 3:4], ksc[l].ap()[0:32, :])
        LDMA(colA[64:96, 3:4], ksc[l].ap()[32:64, :])
        LDMA(colA[96:128, 3:4], ksc[l].ap()[0:32, :])
        G(lambda e: e.memset(vring[:, :], 1.0))

        ntiles = 7 if first_layer else 6
        for i in range(ntiles):
            tau = 512 * i
            s2 = i % 2
            for c in range(8):
                if i == 0:
                    G(lambda e, cc=c: e.memset(xnb[cc][:, 0:1], 0.0))
                else:
                    G(lambda e, cc=c: e.tensor_copy(xnb[cc][:, 0:1], xnb[cc][:, 512:513]))
            if first_layer:
                for s4 in range(4):
                    ln_group(rd(xk, 512 + tau + 128 * s4, 512 + tau + 128 * s4 + 128, 0, D),
                             xnf, xnb, 1 + 128 * s4, True)
            else:
                for c in range(8):
                    LDMA(xnf[c][:, 1:513], src.ap()[c, :, tau:tau + 512])
                    G(lambda e, cc=c: e.tensor_copy(xnb[cc][:, 1:513], xnf[cc][:, 1:513]))
            for ti in range(4):
                LDMA(tabt[ti][:, :], rd(tabs_in, 0, 128, ti * HBUF + b0 + tau, ti * HBUF + b0 + tau + 512))
                V(lambda e, t2=ti: e.tensor_scalar(tabt[t2][:, :], tabt[t2][:, :], colA[:, t2:t2 + 1], None, OP.mult))

            def rhs_x(c):
                return xnb[c][:, (1 if c < 4 else 0):(513 if c < 4 else 512)]

            def qk(which):
                cb = 0 if which == "q" else 512
                t1_, t2_ = (tabt[0], tabt[1]) if which == "q" else (tabt[2], tabt[3])
                sqd = [resb[0], resb[1], tmpb, sqb]
                for pp in range(4):
                    for c in range(8):
                        T(lambda e, cc=c, q=pp: e.matmul(pbk[q][:, :], w_qkv[cc][:, cb + 128 * q:cb + 128 * q + 128], rhs_x(cc), start=(cc == 0), stop=(cc == 7)))
                for pp in range(4):
                    SC(lambda e, q=pp: e.copy(qA[q][:, :], pbk[q][:, :]))
                for pp in range(4):
                    V(lambda e, q=pp: e.tensor_tensor(sqd[q][:, :], qA[q][:, :], qA[q][:, :], OP.mult))
                for pp in range(4):
                    T(lambda e, q=pp: e.matmul(pbk[0][0:8, :], c_oneh[:, 8 * q:8 * q + 8], sqd[q][:, :], start=(q == 0), stop=(q == 3)))
                SC(lambda e: e.activation(r8a[:, :], pbk[0][0:8, :], AF.Ln, bias=1e-24, scale=1.0))
                SC(lambda e: e.activation(r8a[:, :], r8a[:, :], AF.Exp, bias=0.0, scale=-0.5))
                for pp in range(4):
                    for c in range(8):
                        T(lambda e, cc=c, q=pp: e.matmul(pbk[4 + q][:, :], w_rot[cc][:, cb + 128 * q:cb + 128 * q + 128], rhs_x(cc), start=(cc == 0), stop=(cc == 7)))
                for pp in range(4):
                    T(lambda e, q=pp: e.matmul(pbk[q][:, :], c_bc8[:, 128 * q:128 * q + 128], r8a[:, :], start=True, stop=True))
                for pp in range(4):
                    V(lambda e, q=pp, ta=t1_: e.tensor_tensor(sqd[q][:, :], qA[q][:, :], ta[:, :], OP.mult))
                for pp in range(4):
                    V(lambda e, q=pp, tb=t2_: e.tensor_tensor(xtm[:, 0:512], pbk[4 + q][:, :], tb[:, :], OP.mult))
                    V(lambda e, q=pp: e.tensor_add(sqd[q][:, :], sqd[q][:, :], xtm[:, 0:512]))
                    if which == "q":
                        V(lambda e, q=pp: e.tensor_tensor(qbuf[q][:, :], sqd[q][:, :], pbk[q][:, :], OP.mult))
                    else:
                        V(lambda e, q=pp, sl=s2: e.tensor_tensor(kring[q][:, 512 * sl:512 * sl + 512], sqd[q][:, :], pbk[q][:, :], OP.mult))

            qk("k")
            do_attn = (not first_layer) or (i >= 1)
            if do_attn:
                qk("q")
            for sub in range(4):
                for c in range(8):
                    off = (1 if c < 4 else 0) + 128 * sub
                    for pp in range(4):
                        T(lambda e, cc=c, s=sub, q=pp, o=off: e.matmul(pbk[4 + q][:, 128 * s:128 * s + 128], xnb[cc][:, o:o + 128], w_qkv[cc][:, 1024 + 128 * q:1024 + 128 * q + 128], start=(cc == 0), stop=(cc == 7)))
            for pp in range(4):
                dst520 = vring[:, 520 * s2:520 * s2 + 520] if False else None
                V(lambda e, q=pp, s=s2: e.tensor_copy(
                    kring[q][:, :][:, 0:0] if False else
                    vring[:, 520 * (2 * q + s):520 * (2 * q + s) + 520].rearrange("p (s4 h d) -> p s4 h d", s4=4, h=2, d=65)[:, :, :, 0:64],
                    pbk[4 + q][:, :].rearrange("p (s4 h d) -> p s4 h d", s4=4, h=2, d=64)))
            if (not first_layer):
                LDMA(c_vm[:, :], vmask.ap()[:, tau:tau + 512])
            if do_attn:
                clo = max(0, 4 * i - 4)
                chunks = list(range(clo, 4 * i + 4))
                for hh in range(8):
                    pp = hh // 2
                    half = hh % 2
                    h0 = 64 * half
                    for j0 in range(0, len(chunks), 4):
                        grp = chunks[j0:j0 + 4]
                        for j, ck in enumerate(grp):
                            T(lambda e, q=pp, hz=h0, sl=(ck // 4) % 2, k2=ck % 4, b=j:
                              e.matmul(pbk[b][:, :], kring[q][hz:hz + 64, 512 * sl + 128 * k2:512 * sl + 128 * k2 + 128], qbuf[q][hz:hz + 64, :], start=True, stop=True))
                        for j, ck in enumerate(grp):
                            SC(lambda e, b=j, bc=ck + b0 // 128: e.activation(ptile[b][:, :], pbk[b][:, :], AF.Exp, bias=c_fb[:, bc:bc + 1], scale=8.0))
                        for j, ck in enumerate(grp):
                            if ck >= 4 * i:
                                G(lambda e, b=j, bs=tau - 128 * ck: e.affine_select(out=ptile[b][:, :], in_=ptile[b][:, :], pattern=[[1, 512]], compare_op=OP.is_ge, fill=0.0, base=bs, channel_multiplier=-1))
                            else:
                                G(lambda e, b=j, bs=128 * ck - tau + 512: e.affine_select(out=ptile[b][:, :], in_=ptile[b][:, :], pattern=[[-1, 512]], compare_op=OP.is_ge, fill=0.0, base=bs, channel_multiplier=1))
                        for j, ck in enumerate(grp):
                            T(lambda e, q=pp, sl=(ck // 4) % 2, k2=ck % 4, h=half, b=j, st=(j0 == 0 and j == 0), sp=(j0 + j == len(chunks) - 1):
                              e.matmul(pbk[7][0:65, :], vring[:, 520 * (2 * q + sl) + 130 * k2 + 65 * h:520 * (2 * q + sl) + 130 * k2 + 65 * h + 65], ptile[b][:, :], start=st, stop=sp))
                    V(lambda e: e.tensor_copy(r_a[:, :], pbk[7][64:65, :]))
                    SC(lambda e: e.activation(r_b[:, :], r_a[:, :], AF.Ln, bias=1e-30, scale=1.0))
                    SC(lambda e: e.activation(r_b[:, :], r_b[:, :], AF.Exp, bias=0.0, scale=-1.0))
                    T(lambda e: e.matmul(pbk[6][0:64, :], c_1x[0:1, 0:64], r_b[:, :], start=True, stop=True))
                    V(lambda e: e.tensor_copy(rbsb[:, :], pbk[6][0:64, :]))
                    V(lambda e, c2=pp, h=half: e.tensor_tensor(ofb[c2][64 * h:64 * h + 64, :], pbk[7][0:64, :], rbsb[:, :], OP.mult))
                for c in range(8):
                    for ic in range(4):
                        T(lambda e, c2=ic, cc=c, b=c: e.matmul(pbk[b][:, :], w_wo[c2][:, 128 * cc:128 * cc + 128], ofb[c2][:, :], start=(c2 == 0), stop=(c2 == 3)))
                for c in range(8):
                    rs = c % 2
                    V(lambda e, cc=c, b=c, r2=rs: e.tensor_add(resb[r2][:, :], pbk[b][:, :], xnf[cc][:, 1:513]))
                    if not first_layer:
                        V(lambda e, r2=rs: e.tensor_tensor(resb[r2][:, :], resb[r2][:, :], c_vm[:, :], OP.mult))
                    SDMA(dst.ap()[c, :, tau - (512 if first_layer else 0):tau - (512 if first_layer else 0) + 512], resb[rs][:, :])

    def mlp(l, srcscr, hnscr, prtscr, dstscr, jt, scol0, mask_dst):
        LDMA(lg_s[:, :], lgrow[l].ap()[:, :])
        LDMA(lb_s[:, :], lbrow[l].ap()[:, :])
        LDMA(b2_s[:, :], b2cin[l].ap()[:, :])
        LDMA(b1_s[:, :], b1c[l].ap()[:, :])
        for j in range(jt):
            sc = scol0 + 512 * j
            for c in range(8):
                LDMA(xnf[c][:, 1:513], srcscr.ap()[c, :, sc:sc + 512])
                if sc == 0:
                    G(lambda e, cc=c: e.memset(xnf[cc][:, 0:1], 0.0))
                else:
                    LDMA(xnf[c][:, 0:1], srcscr.ap()[c, :, sc - 1:sc], slow=True)

            def hsl(c):
                return xnf[c][:, (1 if c < 4 else 0):(513 if c < 4 else 512)]

            for c in range(8):
                V(lambda e, cc=c: e.tensor_tensor(sqb[:, :], hsl(cc), hsl(cc), OP.mult))
                T(lambda e, cc=c: e.matmul(pbk[2][0:1, :], c_1col[:, :], hsl(cc), start=(cc == 0), stop=(cc == 7)))
                T(lambda e, cc=c: e.matmul(pbk[5][0:1, :], c_1col[:, :], sqb[:, :], start=(cc == 0), stop=(cc == 7)))
            V(lambda e: e.tensor_scalar(r_m[:, :], pbk[2][0:1, :], 1.0 / D, None, OP.mult))
            V(lambda e: e.tensor_tensor(r_a[:, :], r_m[:, :], r_m[:, :], OP.mult))
            V(lambda e: e.scalar_tensor_tensor(r_a[:, :], pbk[5][0:1, :], 1.0 / D, r_a[:, :], OP.mult, OP.subtract))
            SC(lambda e: e.activation(r_b[:, :], r_a[:, :], AF.Ln, bias=1e-5, scale=1.0))
            SC(lambda e: e.activation(r_b[:, :], r_b[:, :], AF.Exp, bias=0.0, scale=-0.5))
            V(lambda e: e.scalar_tensor_tensor(r_a[:, :], r_m[:, :], -1.0, r_b[:, :], OP.mult, OP.mult))
            for c in range(8):
                T(lambda e, cc=c: e.matmul(pbk[3][:, :], lg_s[0:1, 128 * cc:128 * cc + 128], r_b[:, :], start=True, stop=True))
                T(lambda e, cc=c: e.matmul(pbk[4][:, :], lg_s[0:1, 128 * cc:128 * cc + 128], r_a[:, :], start=True, stop=False))
                T(lambda e, cc=c: e.matmul(pbk[4][:, :], lb_s[0:1, 128 * cc:128 * cc + 128], c_orow[:, :], start=False, stop=True))
                V(lambda e, cc=c: e.tensor_tensor(tmpb[:, :], hsl(cc), pbk[3][:, :], OP.mult))
                V(lambda e: e.tensor_add(sqb[:, :], tmpb[:, :], pbk[4][:, :]))
                G(lambda e, cc=c: e.tensor_copy(hnb[cc][:, :], sqb[:, :]))
                SDMA(hnscr.ap()[c, :, 512 * j:512 * j + 512], hnb[c][:, :])
        for half in range(2):
            for c in range(8):
                LDMA(stgF[:, 0:2816], rd(w1p[l], 128 * c, 128 * c + 128, 2816 * half, 2816 * half + 2816))
                G(lambda e, cc=c: e.tensor_copy(w1h[cc][:, :], stgF[:, 0:2816]))
            for mc in range(11):
                LDMA(stgF[:, 0:1024], rd(w2p[l], 128 * (11 * half + mc), 128 * (11 * half + mc) + 128, 0, D))
                G(lambda e, m=mc: e.tensor_copy(w2h[:, 1024 * m:1024 * m + 1024], stgF[:, 0:1024]))
            for j in range(jt):
                sc = scol0 + 512 * j
                if half == 1 and mask_dst:
                    LDMA(c_vm[:, :], vmask.ap()[:, 512 * j:512 * j + 512])
                for c in range(8):
                    LDMA(hnb[c][:, :], hnscr.ap()[c, :, 512 * j:512 * j + 512])
                for mc0 in range(0, 11, 2):
                    grp = [m for m in (mc0, mc0 + 1) if m < 11]
                    for gi, mc in enumerate(grp):
                        for c in range(8):
                            T(lambda e, cc=c, m=mc, b=2 * gi: e.matmul(pbk[b][:, :], w1h[cc][:, 256 * m:256 * m + 128], hnb[cc][:, :], start=(cc == 0), stop=(cc == 7)))
                        for c in range(8):
                            T(lambda e, cc=c, m=mc, b=2 * gi + 1: e.matmul(pbk[b][:, :], w1h[cc][:, 256 * m + 128:256 * m + 256], hnb[cc][:, :], start=(cc == 0), stop=(cc == 7)))
                    for gi, mc in enumerate(grp):
                        ma = 2 * (11 * half + mc)
                        SC(lambda e, m2=ma, b=2 * gi + 1, g2=gi: e.activation(gelb[:, 0:512] if g2 == 0 else gel2[:, 0:512], pbk[b][:, :], AF.Gelu, bias=b1_s[:, m2 + 1:m2 + 2], scale=1.0))
                    for gi, mc in enumerate(grp):
                        ma = 2 * (11 * half + mc)
                        V(lambda e, m=mc, m2=ma, b=2 * gi, g2=gi: e.scalar_tensor_tensor(prodb[m][:, :], pbk[b][:, :], b1_s[:, m2:m2 + 1], gelb[:, 0:512] if g2 == 0 else gel2[:, 0:512], OP.add, OP.mult))
                for c in range(8):
                    bk = 4 + (c % 4)
                    for mc2 in range(11):
                        T(lambda e, m=mc2, cc=c, b=bk: e.matmul(pbk[b][:, :], w2h[:, 1024 * m + 128 * cc:1024 * m + 128 * cc + 128], prodb[m][:, :], start=(m == 0), stop=(m == 10)))
                    ss = c % 2
                    if half == 0:
                        V(lambda e, b=bk, s=ss: e.tensor_copy(resb[s][:, :], pbk[b][:, :]))
                        SDMA(prtscr.ap()[c, :, 512 * j:512 * j + 512], resb[ss][:, :])
                    else:
                        LDMA(tmpb[:, :], prtscr.ap()[c, :, 512 * j:512 * j + 512])
                        LDMA(sqb[:, :], srcscr.ap()[c, :, sc:sc + 512])
                        V(lambda e, b=bk, s=ss: e.tensor_add(resb[s][:, :], pbk[b][:, :], tmpb[:, :]))
                        V(lambda e, s=ss: e.tensor_add(resb[s][:, :], resb[s][:, :], sqb[:, :]))
                        V(lambda e, s=ss, cc=c: e.tensor_scalar(resb[s][:, :], resb[s][:, :], b2_s[:, cc:cc + 1], None, OP.add))
                        if mask_dst:
                            V(lambda e, s=ss: e.tensor_tensor(resb[s][:, :], resb[s][:, :], c_vm[:, :], OP.mult))
                        SDMA(dstscr.ap()[c, :, 512 * j:512 * j + 512], resb[ss][:, :])

    def attn_phase():
        G(lambda e: e.memset(vall[:, :], 1.0))
        for ph in range(2):
            for c in range(8):
                LDMA(stgF[:, 0:512], rd(wv_att, 128 * c, 128 * c + 128, 512 * ph, 512 * ph + 512))
                G(lambda e, cc=c: e.tensor_copy(wv_h[cc][:, :], stgF[:, 0:512]))
            for tg in range(8):
                if ph == 0:
                    for s4 in range(4):
                        ln_group(rd(xk, 512 * tg + 128 * s4, 512 * tg + 128 * s4 + 128, 0, D), None, xnt8, 128 * s4, False)
                    for c in range(8):
                        SDMA(xns.ap()[c, :, 512 * tg:512 * tg + 512], xnt8[c][:, :])
                else:
                    for c in range(8):
                        LDMA(xnt8[c][:, :], xns.ap()[c, :, 512 * tg:512 * tg + 512])
                for sub in range(4):
                    for c in range(8):
                        T(lambda e, cc=c, s=sub: e.matmul(pbk[4][:, :], xnt8[cc][:, 128 * s:128 * s + 128], wv_h[cc][:, :], start=(cc == 0), stop=(cc == 7)))
                    tc = 4 * tg + sub
                    V(lambda e, t2=tc, p2=ph: e.tensor_copy(
                        vall[:, 1040 * t2 + 520 * p2:1040 * t2 + 520 * p2 + 520].rearrange("p (q h d) -> p q h d", q=4, h=2, d=65)[:, :, :, 0:64],
                        pbk[4][:, :].rearrange("p (q h d) -> p q h d", q=4, h=2, d=64)))
        for pp in range(8):
            for c in range(8):
                LDMA(stgF[:, 0:128], rd(wq_att, 128 * c, 128 * c + 128, 128 * pp, 128 * pp + 128))
                LDMA(stgF[:, 128:256], rd(wk_att, 128 * c, 128 * c + 128, 128 * pp, 128 * pp + 128))
                G(lambda e, cc=c: e.tensor_copy(wqk_a[cc][:, :], stgF[:, 0:256]))
            for kt in range(8):
                for c in range(8):
                    LDMA(xnt8[c][:, :], xns.ap()[c, :, 512 * kt:512 * kt + 512])
                for c in range(8):
                    T(lambda e, cc=c: e.matmul(pbk[0][:, :], wqk_a[cc][:, 128:256], xnt8[cc][:, :], start=(cc == 0), stop=(cc == 7)))
                SC(lambda e, k2=kt: e.copy(kpr[:, 512 * k2:512 * k2 + 512], pbk[0][:, :]))
                if kt >= 4:
                    for c in range(8):
                        T(lambda e, cc=c: e.matmul(pbk[1][:, :], wqk_a[cc][:, 0:128], xnt8[cc][:, :], start=(cc == 0), stop=(cc == 7)))
                    SC(lambda e, k2=kt - 4: e.copy(qpr[:, 512 * k2:512 * k2 + 512], pbk[1][:, :]))
            for hh in range(2):
                h0 = 64 * hh
                for qt in range(4):
                    kmax = 20 + 4 * qt
                    for j0 in range(0, kmax, 6):
                        grp = list(range(j0, min(j0 + 6, kmax)))
                        for j, ck in enumerate(grp):
                            T(lambda e, k2=ck, b=j, h=h0, q=qt: e.matmul(pbk[b][:, :], kpr[h:h + 64, 128 * k2:128 * k2 + 128], qpr[h:h + 64, 512 * q:512 * q + 512], start=True, stop=True))
                        for j, ck in enumerate(grp):
                            SC(lambda e, b=j, k2=ck: e.activation(aptile[b][:, :], pbk[b][:, :], AF.Exp, bias=c_kb[:, k2:k2 + 1], scale=0.125))
                        for j, ck in enumerate(grp):
                            if ck >= kmax - 4:
                                G(lambda e, b=j, bs=2048 + 512 * qt - 128 * ck: e.affine_select(out=aptile[b][:, :], in_=aptile[b][:, :], pattern=[[1, 512]], compare_op=OP.is_ge, fill=0.0, base=bs, channel_multiplier=-1))
                        for j, ck in enumerate(grp):
                            T(lambda e, k2=ck, h=hh, b=j, q=pp, st=(j0 == 0 and j == 0), sp=(j0 + j == kmax - 1):
                              e.matmul(pbk[7][0:65, :], vall[:, 1040 * k2 + 520 * (q // 4) + 130 * (q % 4) + 65 * h:1040 * k2 + 520 * (q // 4) + 130 * (q % 4) + 65 * h + 65], aptile[b][:, :], start=st, stop=sp))
                    V(lambda e: e.tensor_copy(r_a[:, :], pbk[7][64:65, :]))
                    SC(lambda e: e.activation(r_b[:, :], r_a[:, :], AF.Ln, bias=1e-30, scale=1.0))
                    SC(lambda e: e.activation(r_b[:, :], r_b[:, :], AF.Exp, bias=0.0, scale=-1.0))
                    T(lambda e: e.matmul(pbk[6][0:64, :], c_1x[0:1, 0:64], r_b[:, :], start=True, stop=True))
                    V(lambda e: e.tensor_copy(rbsb[:, :], pbk[6][0:64, :]))
                    V(lambda e, h=h0: e.tensor_tensor(osm[h:h + 64, :], pbk[7][0:64, :], rbsb[:, :], OP.mult))
                    SDMA(oalls.ap()[pp, 64 * hh:64 * hh + 64, 512 * qt:512 * qt + 512], osm[64 * hh:64 * hh + 64, :])
        for qt in range(4):
            for ic in range(8):
                LDMA(oqt[ic][:, :], oalls.ap()[ic, :, 512 * qt:512 * qt + 512])
            for wh in range(2):
                for c2 in range(4):
                    cc = 4 * wh + c2
                    LDMA(stgF[:, 0:1024], rd(wcat_in, 128 * cc, 128 * cc + 128, 0, D))
                    G(lambda e, c3=c2: e.tensor_copy(wcat_s[c3][:, :], stgF[:, 0:1024]))
                for c in range(8):
                    for ic in range(4):
                        T(lambda e, c2=ic, cc=c, b=c, w=wh: e.matmul(pbk[b][:, :], wcat_s[c2][:, 128 * cc:128 * cc + 128], oqt[4 * w + c2][:, :], start=(c2 == 0 and w == 0), stop=(c2 == 3 and w == 1)))
                if wh == 0:
                    continue
                for c in range(8):
                    LDMA(tmpb[:, :], Fs.ap()[c, :, 512 * qt:512 * qt + 512])
                    V(lambda e, b=c, cc=c: e.scalar_tensor_tensor(xnf[cc][:, 0:512], pbk[b][:, :], bo_s[:, cc:cc + 1], tmpb[:, :], OP.add, OP.add))
            for s4 in range(4):
                for dmg in range(2):
                    for c2 in range(4):
                        cc = 4 * dmg + c2
                        T(lambda e, c3=cc, s=s4, b=2 + dmg, c4=c2: e.transpose(pbk[b][:, 128 * c4:128 * c4 + 128], xnf[c3][:, 128 * s:128 * s + 128], c_eyef[:, :]))
                LDMA(xtm[:, :], rd(xk, 2048 + 512 * qt + 128 * s4, 2048 + 512 * qt + 128 * s4 + 128, 0, D))
                V(lambda e: e.tensor_add(resb[0][:, :], pbk[2][:, :], xtm[:, 0:512]))
                V(lambda e: e.tensor_add(resb[1][:, :], pbk[3][:, :], xtm[:, 512:1024]))
                SDMA(out.ap()[512 * qt + 128 * s4:512 * qt + 128 * s4 + 128, 0:512], resb[0][:, :])
                SDMA(out.ap()[512 * qt + 128 * s4:512 * qt + 128 * s4 + 128, 512:1024], resb[1][:, :])

    wv_h = [wv_a[i // 2][:, 512 * (i % 2):512 * (i % 2) + 512] for i in range(8)]

    ffn_attn(0, None, a1s, True)
    mlp(0, a1s, hn1, prt, x2s, 6, 0, True)
    ffn_attn(1, x2s, a2s, False)
    mlp(1, a2s, hn2, prt2, Fs, 4, 1024, False)
    attn_phase()
    p.step("sync", None)

    import contextlib
    stack = contextlib.ExitStack()
    sems = {}
    for s in ("T", "V", "S", "G", "Y", "LD", "ST"):
        sems[s] = stack.enter_context(nc.semaphore(f"sem{s}"))
    blk = stack.enter_context(nc.Block())

    def mk(engname):
        def body(eng):
            for fn, waits, sem, amt in p.streams[engname]:
                for (s, v) in waits:
                    eng.wait_ge(sems[s], v)
                if fn is not None:
                    fn(eng).then_inc(sems[sem], amt)
        return body

    blk.tensor(mk("tensor"))
    blk.vector(mk("vector"))
    blk.scalar(mk("scalar"))
    blk.gpsimd(mk("gpsimd"))
    blk.sync(mk("sync"))
    stack.close()
    return nc


def _tables(core_h):
    toff = 512 if core_h == 1 else -1536
    t = np.arange(HBUF, dtype=np.float64) + toff
    d = np.arange(64)
    half = d % 32
    inv_freq = 1.0 / (10000.0 ** (2 * half / 64.0))
    freqs = t[:, None] * inv_freq[None, :]
    base = (2 * half + 0.4 * 64) / (1.4 * 64)
    power = (t - 2048.0) / 256.0
    xsc = base[None, :] ** power[:, None]
    sgn = np.where(d < 32, -1.0, 1.0)
    t1q = np.cos(freqs) * xsc
    t2q = np.sin(freqs) * xsc * sgn[None, :]
    t1k = np.cos(freqs) / xsc
    t2k = np.sin(freqs) / xsc * sgn[None, :]
    tab = np.concatenate([t1q.T, t2q.T, t1k.T, t2k.T], axis=1).astype(np.float32)
    return np.ascontiguousarray(np.concatenate([tab, tab], axis=0))


_NC_CACHE = {}
_LAST_RES = None


def kernel(L0_attn_resid, L0_attn_W_Q, L0_attn_W_K, L0_attn_W_V, L0_attn_W_out,
           L0_attn_b_out, L0_ffn_wqkv, L0_ffn_q_scale, L0_ffn_k_scale, L0_ffn_wo,
           L0_ffn_ln_g, L0_ffn_ln_b, L0_ffn_w1, L0_ffn_b1, L0_ffn_w2, L0_ffn_b2):
    x = np.ascontiguousarray(np.asarray(L0_attn_resid, np.float32))
    f32 = lambda a: np.ascontiguousarray(np.asarray(a, np.float32))
    sh = {}
    sh["eyef"] = np.eye(128, dtype=np.float32)
    oneh = np.zeros((128, 32), np.float32)
    for pp in range(4):
        oneh[0:64, 8 * pp + 2 * pp] = 1.0
        oneh[64:128, 8 * pp + 2 * pp + 1] = 1.0
    sh["onehots"] = oneh
    sh["onescol"] = np.ones((128, 1), np.float32)
    bc8 = np.zeros((8, 512), np.float32)
    for pp in range(4):
        bc8[2 * pp, 128 * pp:128 * pp + 64] = 1.0
        bc8[2 * pp + 1, 128 * pp + 64:128 * pp + 128] = 1.0
    sh["bcast8"] = bc8
    sh["ones1x"] = np.ones((1, 128), np.float32)
    sh["onesrow"] = np.ones((1, 512), np.float32)
    sh["negio"] = np.tile(-np.arange(512, dtype=np.float32)[None, :], (128, 1))
    sh["constv"] = np.tile(np.array([1e-5, 1e-24, 1e-30, 0.0], np.float32)[None, :], (128, 1))
    wqkv = f32(L0_ffn_wqkv)
    perm = (np.arange(64) + 32) % 64
    for l in range(2):
        sh[f"wbig{l}"] = np.ascontiguousarray(wqkv[l])
        rot = np.zeros((D, 1024), np.float32)
        rot[:, 0:512] = wqkv[l][:, 0:512].reshape(D, 8, 64)[:, :, perm].reshape(D, 512)
        rot[:, 512:1024] = wqkv[l][:, 512:1024].reshape(D, 8, 64)[:, :, perm].reshape(D, 512)
        sh[f"wrot{l}"] = rot
        sh[f"wo{l}"] = f32(L0_ffn_wo)[l]
        w1 = f32(L0_ffn_w1)[l]
        b1 = f32(L0_ffn_b1)[l]
        a = np.zeros((D, MP * 128), np.float32)
        g = np.zeros((D, MP * 128), np.float32)
        a[:, 0:2730] = w1[:, 0:2730]
        g[:, 0:2730] = w1[:, 2730:5460]
        ba = np.zeros(MP * 128, np.float32)
        bg = np.zeros(MP * 128, np.float32)
        ba[0:2730] = b1[0:2730]
        bg[0:2730] = b1[2730:5460]
        w1pp = np.zeros((D, MP * 256), np.float32)
        b1cc = np.zeros((128, 2 * MP), np.float32)
        for m in range(MP):
            w1pp[:, 256 * m:256 * m + 128] = a[:, 128 * m:128 * m + 128]
            w1pp[:, 256 * m + 128:256 * m + 256] = g[:, 128 * m:128 * m + 128]
            b1cc[:, 2 * m] = ba[128 * m:128 * m + 128]
            b1cc[:, 2 * m + 1] = bg[128 * m:128 * m + 128]
        sh[f"w1p{l}"] = w1pp
        sh[f"b1c{l}"] = b1cc
        w2pp = np.zeros((MP * 128, D), np.float32)
        w2pp[0:2730, :] = f32(L0_ffn_w2)[l]
        sh[f"w2p{l}"] = w2pp
        sh[f"b2cin{l}"] = np.ascontiguousarray(f32(L0_ffn_b2)[l].reshape(8, 128).T)
        sh[f"lgrow{l}"] = f32(L0_ffn_ln_g)[l][None, :]
        sh[f"lbrow{l}"] = f32(L0_ffn_ln_b)[l][None, :]
        sh[f"qsc{l}"] = f32(L0_ffn_q_scale)[l][:, None]
        sh[f"ksc{l}"] = f32(L0_ffn_k_scale)[l][:, None]
    sh["wq_att"] = f32(L0_attn_W_Q).reshape(D, D)
    sh["wk_att"] = f32(L0_attn_W_K).reshape(D, D)
    sh["wv_att"] = f32(L0_attn_W_V).reshape(D, D)
    sh["wcat"] = np.ascontiguousarray(f32(L0_attn_W_out).transpose(1, 0, 2).reshape(D, D))
    sh["bocin"] = np.ascontiguousarray(f32(L0_attn_b_out).reshape(8, 128).T)

    tab_lo, tab_hi = _tables(0), _tables(1)
    kb_lo = np.full((128, 32), -24.0, np.float32)
    kb_lo[:, 0:16] = -1e9
    kb_hi = np.full((128, 32), -24.0, np.float32)
    fb_lo = np.full((128, 28), -32.0, np.float32)
    fb_lo[:, 0:12] = -1e9
    fb_hi = np.full((128, 28), -32.0, np.float32)
    vl_lo = np.full((128, 1), -1535.5, np.float32)
    vl_hi = np.full((128, 1), 1e9, np.float32)

    if "nc" not in _NC_CACHE:
        _NC_CACHE["nc"] = build_nc()
    nc = _NC_CACHE["nc"]
    in_maps = []
    for core in range(8):
        b, h = core // 2, core % 2
        xkc = x[b] if h == 1 else np.concatenate([np.zeros((2048, D), np.float32), x[b, 0:2048]], axis=0)
        vm = np.ones((128, 3072), np.float32)
        if h == 0:
            vm[:, 0:1024] = 0.0
        m = {"vmask": vm,
             "xk": np.ascontiguousarray(xkc),
             "kbias": kb_hi if h == 1 else kb_lo,
             "fbias": fb_hi if h == 1 else fb_lo,
             "tabs": tab_hi if h == 1 else tab_lo,
             "vlen": vl_hi if h == 1 else vl_lo}
        m.update(sh)
        in_maps.append(m)
    res = run_bass_kernel_spmd(nc, in_maps, core_ids=list(range(8)))
    global _LAST_RES
    _LAST_RES = res
    o = np.zeros((B, S, D), np.float32)
    for core in range(8):
        b, h = core // 2, core % 2
        o[b, 2048 * h:2048 * h + 2048] = res.results[core]["out"]
    return o
